# version 23
# baseline (speedup 1.0000x reference)
import contextlib
import numpy as np
import concourse.bass as bass
import concourse.mybir as mybir
from concourse.bass_utils import run_bass_kernel_spmd

F32 = mybir.dt.float32
BF16 = mybir.dt.bfloat16
AF = mybir.ActivationFunctionType
ALU = mybir.AluOpType
AX = mybir.AxisListType.X

PI = float(np.pi)
C2PI = float(np.float32(2.0 * np.pi))
MAGIC = 3.0 * 4194304.0
SIN_BIAS = float(np.float32(np.pi / 2))
RR_HI = float(np.float32(np.pi - 5e-5 - np.pi / 2))
RR_LO = float(np.float32(-np.pi + 5e-5 - np.pi / 2))
EPS = 1e-5

D = 1024
FF = 4096
E = 8
R = 16
NQ = 16
PC = 2048
T = 512
G = PC // T
NC_ = 4
NKD = D // 128
NJF = FF // 128
NKF = FF // 128

OFF_WG = 0
OFF_WU = 1024
OFF_B1 = 2048
OFF_B3 = 3072
OFF_A2 = 4096
WJ_COLS = 4096 + 256

OFF_A1 = 0
OFF_A3 = 1024
OFF_B2LO = 2048
OFF_B2HI = 3072
OFF_GATE = 4096
SP_COLS = 4096 + 64

C_EPS = 0
C_PIH = 1
C_QTR = 2
C_MAG = 3
C_NMAG = 4


class Sched:
    def __init__(self):
        self.items = {k: [] for k in ("sync", "gpsimd", "tensor", "vector", "scalar")}
        self.cnt = {}

    def ev(self, sem, amt=1):
        self.cnt[sem] = self.cnt.get(sem, 0) + amt
        return (sem, self.cnt[sem])

    def cur(self, sem):
        return (sem, self.cnt.get(sem, 0))

    def op(self, engine, fn, waits=(), inc=None):
        ev = self.ev(inc, 1) if inc is not None else None
        self.items[engine].append((fn, list(waits), ev))
        return ev

    def dma(self, engine, fn, sem, waits=()):
        ev = self.ev(sem, 16)
        self.items[engine].append(
            (lambda e, f=fn, s=sem: f(e).then_inc(s, 16), list(waits), None))
        return ev

    def emit(self, engine, eng):
        for fn, waits, ev in self.items[engine]:
            for (sem, val) in waits:
                eng.wait_ge(sem, val)
            inst = fn(eng)
            if ev is not None:
                inst.then_inc(ev[0], 1)


def build(debug=False):
    nc = bass.Bass("TRN2", target_bir_lowering=False, detect_race_conditions=False)

    xd = nc.dram_tensor("x", [PC, D], F32, kind="ExternalInput").ap()
    wjp = nc.dram_tensor("wjp", [NJF, 128, WJ_COLS], F32, kind="ExternalInput").ap()
    w2p = nc.dram_tensor("w2p", [NKF, 128, 1024], F32, kind="ExternalInput").ap()
    wdp = nc.dram_tensor("wdp", [NKF, 128, 1024], F32, kind="ExternalInput").ap()
    spk = nc.dram_tensor("spk", [128, SP_COLS], F32, kind="ExternalInput").ap()
    w1t = nc.dram_tensor("w1t", [16, FF], F32, kind="ExternalInput").ap()
    b1d = nc.dram_tensor("b1d", [128, 32], F32, kind="ExternalInput").ap()
    b2d = nc.dram_tensor("b2d", [128, D], F32, kind="ExternalInput").ap()
    g1d = nc.dram_tensor("g1d", [128, D], F32, kind="ExternalInput").ap()
    be1d = nc.dram_tensor("be1d", [128, D], F32, kind="ExternalInput").ap()
    g2d = nc.dram_tensor("g2d", [128, D], F32, kind="ExternalInput").ap()
    be2d = nc.dram_tensor("be2d", [128, D], F32, kind="ExternalInput").ap()
    idfd = nc.dram_tensor("idfd", [128, 128], F32, kind="ExternalInput").ap()
    seld = nc.dram_tensor("seld", [8, 1024], F32, kind="ExternalInput").ap()
    gtd = nc.dram_tensor("gtd", [128, 64], F32, kind="ExternalInput").ap()
    cstd = nc.dram_tensor("cstd", [128, 8], F32, kind="ExternalInput").ap()
    od = nc.dram_tensor("o", [PC, D], F32, kind="ExternalOutput").ap()
    wdq = nc.dram_tensor("wdq", [NKF, 128, 1024], BF16, kind="Internal").ap()
    w2q = nc.dram_tensor("w2q", [NKF, 128, 1024], BF16, kind="Internal").ap()

    dbg = {}
    if debug:
        for nm, shp, dt in [
            ("dbg_xn", [128, NC_, D], F32), ("dbg_x1", [128, NC_, D], F32),
            ("dbg_s", [128, NC_, D], F32), ("dbg_probs", [128, NC_, 8], F32),
            ("dbg_wgt", [128, NC_, 8], F32), ("dbg_cw1", [128, T], F32),
            ("dbg_cw3", [128, T], F32), ("dbg_u1", [128, T], F32),
            ("dbg_sact", [128, T], F32), ("dbg_mix", [128, NJF, T], F32),
            ("dbg_zlo", [128, T], F32), ("dbg_zhi", [128, T], F32),
            ("dbg_ff", [128, NKD, T], F32), ("dbg_g", [128, NKD, T], F32),
            ("dbg_ao", [128, NC_, D], F32), ("dbg_q", [128, NC_, NQ], F32),
            ("dbg_t1lo", [128, T], F32), ("dbg_rt", [128, NJF, T], F32),
        ]:
            dbg[nm] = nc.dram_tensor(nm, shp, dt, kind="ExternalOutput").ap()

    S = Sched()
    es = contextlib.ExitStack()
    with es:
        def sb(name, shape, dt):
            return es.enter_context(nc.sbuf_tensor(name, shape, dt))

        cst = sb("cst", [128, 8], F32)
        b1c = sb("b1c", [128, 32], F32)
        b2bc = sb("b2bc", [128, D], F32)
        g1b = sb("g1b", [128, D], F32)
        be1b = sb("be1b", [128, D], F32)
        g2b = sb("g2b", [128, D], F32)
        be2b = sb("be2b", [128, D], F32)
        idf = sb("idf", [128, 128], F32)
        idb = sb("idb", [128, 128], BF16)
        selb = sb("selb", [8, 1024], BF16)
        smb = sb("smb", [128, SP_COLS], BF16)
        w1b = sb("w1b", [16, FF], BF16)

        xr = sb("xr", [128, NC_, D], F32)
        xn = sb("xn", [128, NC_, D], F32)
        sfin = sb("sfin", [128, NC_, D], F32)
        scrA = sb("scrA", [128, NC_, D], F32)
        scrB = sb("scrB", [128, NC_, D], F32)
        xnT = sb("xnT", [128, NKD, T], BF16)
        big = sb("big", [128, NJF, T], BF16)
        stats = sb("stats", [128, 16], F32)
        gprob = sb("gprob", [128, NC_, 8], F32)
        gwc = sb("gwc", [128, NC_, 8], F32)
        mx8c = sb("mx8c", [128, NC_, 8], F32)
        wgt = sb("wgt", [128, NC_, 8], F32)
        wgtT = sb("wgtT", [8, T], BF16)
        wbc = sb("wbc", [128, E, T], BF16)
        qT = sb("qT", [16, T], BF16)
        gtf = sb("gtf", [128, 64], F32)
        xnf = sb("xnf", [128, 3, 128], F32)
        qs1 = sb("qs1", [128, NC_, NQ], F32)
        qs2 = sb("qs2", [128, NC_, NQ], F32)
        cw1s = sb("cw1s", [128, 2, T], BF16)
        cw3s = sb("cw3s", [128, 2, T], BF16)
        u1b = sb("u1b", [128, 2, T], BF16)
        u3b = sb("u3b", [128, 2, T], BF16)
        sab = sb("sab", [128, 3, T], BF16)
        t1s = sb("t1s", [128, T], BF16)
        t3s = sb("t3s", [128, T], BF16)
        zlos = sb("zlos", [128, T], BF16)
        zhis = sb("zhis", [128, T], BF16)
        wjb = sb("wjb", [128, 2, WJ_COLS], BF16)
        wsb = sb("wsb", [128, 2, 1024], BF16)

        pb = [es.enter_context(nc.psum_tensor(f"pb{i}", [128, 512], F32)) for i in range(8)]

        sy = es.enter_context(nc.semaphore("sy"))
        sg = es.enter_context(nc.semaphore("sg"))
        sgc = es.enter_context(nc.semaphore("sgc"))
        st = es.enter_context(nc.semaphore("st"))
        sv = es.enter_context(nc.semaphore("sv"))
        ss = es.enter_context(nc.semaphore("ss"))
        so = es.enter_context(nc.semaphore("so"))
        sqw = es.enter_context(nc.semaphore("sqw"))
        sx = es.enter_context(nc.semaphore("sx"))
        sd = es.enter_context(nc.semaphore("sd"))

        def YD(fn, waits=()):
            return S.dma("sync", fn, sy, waits)

        def OD(fn, waits=()):
            return S.dma("sync", fn, so, waits)

        def GD(fn, waits=()):
            return S.dma("gpsimd", fn, sg, waits)

        def QW(fn, waits=()):
            return S.dma("gpsimd", fn, sqw, waits)

        def XD(fn, waits=()):
            return S.dma("gpsimd", fn, sx, waits)

        def SD(fn, waits=()):
            return S.dma("scalar", fn, sd, waits)

        def GC(fn, waits=(), inc=True):
            return S.op("gpsimd", fn, waits, sgc if inc else None)

        def TT(fn, waits=(), inc=True):
            return S.op("tensor", fn, waits, st if inc else None)

        def V(fn, waits=(), inc=True):
            return S.op("vector", fn, waits, sv if inc else None)

        def SC(fn, waits=(), inc=True):
            return S.op("scalar", fn, waits, ss if inc else None)

        YD(lambda e: e.dma_start(out=cst[:, :], in_=cstd))
        YD(lambda e: e.dma_start(out=b1c[:, :], in_=b1d))
        YD(lambda e: e.dma_start(out=b2bc[:, :], in_=b2d))
        YD(lambda e: e.dma_start(out=g1b[:, :], in_=g1d))
        YD(lambda e: e.dma_start(out=be1b[:, :], in_=be1d))
        YD(lambda e: e.dma_start(out=g2b[:, :], in_=g2d))
        YD(lambda e: e.dma_start(out=be2b[:, :], in_=be2d))
        YD(lambda e: e.dma_start(out=idf[:, :], in_=idfd))
        YD(lambda e: e.dma_start(out=gtf[:, :], in_=gtd))
        GD(lambda e: e.dma_start(out=idb[:, :], in_=idfd))
        GD(lambda e: e.dma_start(out=selb[:, :], in_=seld))
        GD(lambda e: e.dma_start(out=smb[:, :], in_=spk))
        GD(lambda e: e.dma_start(out=w1b[:, :], in_=w1t))
        ev_const_y = S.cur(sy)
        ev_const_g = S.cur(sg)

        def _conv_gen():
            for ch in range(8):
                QW(lambda e, ch=ch: e.dma_start(
                    out=wdq[4 * ch:4 * ch + 4], in_=wdp[4 * ch:4 * ch + 4]))
                yield
            for ch in range(8):
                QW(lambda e, ch=ch: e.dma_start(
                    out=w2q[4 * ch:4 * ch + 4], in_=w2p[4 * ch:4 * ch + 4]))
                yield

        conv_state = {"gen": _conv_gen(), "live": True}
        CONV_WDW2_DONE = (sqw, 16 * 16)

        ev_zlo_prev = None
        ev_zhi_prev = None
        ev_odma_prev = None
        ev_outv_prev = None
        ev_scrb_guard = None
        ev_banks67_prev = None
        ev_wsb_guard = None
        ev_wj_used = [None, None]

        for g in range(G):
            x1 = xn
            rows = xd[g * T:(g + 1) * T, :].rearrange("(p c) d -> p c d", c=NC_)
            w_x = [] if ev_outv_prev is None else [ev_outv_prev]
            ev_x = XD(lambda e, r=rows: e.dma_start(out=xr[:, :, :], in_=r), waits=w_x)
            wait_in = [ev_x, ev_const_y, ev_const_g]

            for c in range(NC_):
                V(lambda e, c=c: e.scalar_tensor_tensor(xn[:, c, :], xr[:, c, :], 1.0, xr[:, c, :],
                                                        ALU.mult, ALU.mult,
                                                        accum_out=stats[:, 8 + c:9 + c]),
                  waits=wait_in if c == 0 else ())
            for c in range(NC_):
                V(lambda e, c=c: e.reduce_sum(stats[:, c:c + 1], xr[:, c, :], axis=AX))
            for c in range(NC_):
                V(lambda e, c=c: e.tensor_scalar(stats[:, 4 + c:5 + c], stats[:, c:c + 1],
                                                 1.0 / D, None, ALU.mult))
            for c in range(NC_):
                V(lambda e, c=c: e.scalar_tensor_tensor(stats[:, 12 + c:13 + c],
                                                        stats[:, 4 + c:5 + c], -float(D),
                                                        stats[:, 4 + c:5 + c], ALU.mult, ALU.mult))
            for c in range(NC_):
                V(lambda e, c=c: e.tensor_add(stats[:, 8 + c:9 + c], stats[:, 8 + c:9 + c],
                                              stats[:, 12 + c:13 + c]))
            ev_var = S.cur(sv)
            for c in range(NC_):
                SC(lambda e, c=c: e.activation(stats[:, 12 + c:13 + c], stats[:, 8 + c:9 + c],
                                               AF.Sqrt, bias=cst[:, C_EPS:C_EPS + 1], scale=1.0 / D),
                   waits=[ev_var] if c == 0 else ())
            ev_sd = S.cur(ss)
            for c in range(NC_):
                V(lambda e, c=c: e.reciprocal(stats[:, 12 + c:13 + c], stats[:, 12 + c:13 + c]),
                  waits=[ev_sd] if c == 0 else ())
            V(lambda e: e.drain(), inc=False)
            ev_xn_c = []
            for c in range(NC_):
                ev_xn_c.append(V(lambda e, c=c: e.tensor_scalar(
                    xn[:, c, :], xr[:, c, :],
                    stats[:, 4 + c:5 + c], stats[:, 12 + c:13 + c],
                    ALU.subtract, ALU.mult)))
            ev_xn = S.cur(sv)
            if debug and g == 0:
                OD(lambda e: e.dma_start(out=dbg["dbg_xn"], in_=xn[:, :, :]), waits=[ev_xn])

            ev_tr = []
            ev_gcopy = []
            ev_gm_hist = []
            ev_f_hist = []

            def emit_gate_mm(i2):
                c2, kd2 = divmod(i2, NKD)
                gb2 = 4 + c2 % 2
                w_g = [ev_f_hist[i2]]
                if kd2 == 0 and c2 >= 2:
                    w_g.append(ev_gcopy[c2 - 2])
                ev_gm = TT(lambda e, kd=kd2, gb=gb2, par=i2 % 3:
                           e.matmul(pb[gb][0:128, 0:8], xnf[:, par, :],
                                    gtf[:, kd * 8:(kd + 1) * 8],
                                    start=(kd == 0), stop=(kd == NKD - 1)),
                           waits=w_g)
                ev_gm_hist.append(ev_gm)
                if kd2 == NKD - 1:
                    ev_cp = V(lambda e, c=c2, gb=gb2:
                              e.tensor_copy(gprob[:, c, :], pb[gb][0:128, 0:8]),
                              waits=[S.cur(st)])
                    ev_gcopy.append(ev_cp)

            for c in range(NC_):
                for kd in range(NKD):
                    i = c * NKD + kd
                    b = (2, 3, 6, 7)[i % 4]
                    par = i % 3
                    waits = [ev_xn_c[c]] if kd == 0 else []
                    if i in (2, 3) and ev_banks67_prev is not None:
                        waits.append(ev_banks67_prev)
                    if len(ev_tr) >= 4:
                        waits.append(ev_tr[-4][1])
                    ev_t = TT(lambda e, c=c, kd=kd, b=b:
                              e.transpose(pb[b][0:128, 0:128],
                                          xn[:, c, kd * 128:(kd + 1) * 128], idf[:, :]),
                              waits=waits)
                    ev_e = V(lambda e, c=c, kd=kd, b=b:
                             e.tensor_copy(xnT[:, kd, c * 128:(c + 1) * 128], pb[b][0:128, 0:128]),
                             waits=[ev_t])
                    w_f = []
                    if i >= 3:
                        w_f.append(ev_gm_hist[i - 3])
                    ev_f = V(lambda e, b=b, par=par:
                             e.tensor_copy(xnf[:, par, :], pb[b][0:128, 0:128]),
                             waits=w_f)
                    ev_f_hist.append(ev_f)
                    ev_tr.append((ev_e, ev_f))
                    if i >= 2:
                        emit_gate_mm(i - 2)
            emit_gate_mm(30)
            emit_gate_mm(31)
            ev_xnT = S.cur(sv)

            ev_glog = ev_gcopy[-1]
            for c in range(NC_):
                V(lambda e, c=c: e.reduce_max(stats[:, c:c + 1], gprob[:, c, :], axis=AX),
                  waits=[ev_glog] if c == 0 else ())
            V(lambda e: e.drain(), inc=False)
            for c in range(NC_):
                V(lambda e, c=c: e.tensor_scalar(stats[:, 4 + c:5 + c], stats[:, c:c + 1],
                                                 -1.0, None, ALU.mult))
            ev_gmax = S.cur(sv)
            for c in range(NC_):
                SC(lambda e, c=c: e.activation(gprob[:, c, :], gprob[:, c, :], AF.Exp,
                                               bias=stats[:, 4 + c:5 + c], scale=1.0),
                   waits=[ev_gmax] if c == 0 else ())
            ev_gexp = S.cur(ss)
            for c in range(NC_):
                V(lambda e, c=c: e.reduce_sum(stats[:, 8 + c:9 + c], gprob[:, c, :], axis=AX),
                  waits=[ev_gexp] if c == 0 else ())
            V(lambda e: e.drain(), inc=False)
            for c in range(NC_):
                V(lambda e, c=c: e.reciprocal(stats[:, 12 + c:13 + c], stats[:, 8 + c:9 + c]))
            V(lambda e: e.drain(), inc=False)
            for c in range(NC_):
                V(lambda e, c=c: e.tensor_scalar(gprob[:, c, :], gprob[:, c, :],
                                                 stats[:, 12 + c:13 + c], None, ALU.mult))
            V(lambda e: e.drain(), inc=False)
            ev_ptap = None
            if debug and g == 0:
                ev_p_ = S.cur(sv)
                ev_ptap = OD(lambda e: e.dma_start(out=dbg["dbg_probs"], in_=gprob[:, :, :]), waits=[ev_p_])
            for c in range(NC_):
                V(lambda e, c=c: e.max(out=mx8c[:, c, :], in_=gprob[:, c, :]),
                  waits=[ev_ptap] if (c == 0 and ev_ptap is not None) else ())
            V(lambda e: e.drain(), inc=False)
            for c in range(NC_):
                V(lambda e, c=c: e.tensor_add(stats[:, c:c + 1], mx8c[:, c, 0:1], mx8c[:, c, 1:2]))
            for c in range(NC_):
                V(lambda e, c=c: e.tensor_scalar(gwc[:, c, :], gprob[:, c, :], mx8c[:, c, 1:2],
                                                 -1e30, ALU.subtract, ALU.mult))
            V(lambda e: e.drain(), inc=False)
            for c in range(NC_):
                V(lambda e, c=c: e.reciprocal(stats[:, 4 + c:5 + c], stats[:, c:c + 1]))
            for c in range(NC_):
                V(lambda e, c=c: e.tensor_scalar(gwc[:, c, :], gwc[:, c, :], 1.0, 0.0, ALU.min, ALU.max))
            V(lambda e: e.drain(), inc=False)
            for c in range(NC_):
                V(lambda e, c=c: e.tensor_scalar(gwc[:, c, :], gwc[:, c, :], -1.0, 1.0, ALU.mult, ALU.add))
            for c in range(NC_):
                V(lambda e, c=c: e.tensor_scalar(gprob[:, c, :], gprob[:, c, :],
                                                 stats[:, 4 + c:5 + c], None, ALU.mult))
            V(lambda e: e.drain(), inc=False)
            for c in range(NC_):
                V(lambda e, c=c: e.tensor_mul(wgt[:, c, :], gprob[:, c, :], gwc[:, c, :]))
            V(lambda e: e.drain(), inc=False)
            ev_wgt = S.cur(sv)
            if debug and g == 0:
                OD(lambda e: e.dma_start(out=dbg["dbg_wgt"], in_=wgt[:, :, :]), waits=[ev_wgt])

            wbc_state = {}

            def emit_wgt_t():
                ev_wt = []
                for c in range(NC_):
                    b = 6 + c % 2
                    waits = [ev_wgt, ev_tst] if c == 0 else []
                    if len(ev_wt) >= 2:
                        waits.append(ev_wt[-2])
                    ev_t = TT(lambda e, c=c, b=b: e.transpose(pb[b][0:8, 0:128],
                                                              wgt[:, c, :], idf[:, :]),
                              waits=waits)
                    ev_e = V(lambda e, c=c, b=b: e.tensor_copy(wgtT[:, c * 128:(c + 1) * 128],
                                                               pb[b][0:8, 0:128]),
                             waits=[ev_t])
                    ev_wt.append(ev_e)
                wbc_state["wgtT"] = S.cur(sv)

            def emit_wbc():
                ev_wb = []
                for ee in range(E):
                    b = 6 + ee % 2
                    waits = [wbc_state["wgtT"]] if ee == 0 else []
                    if len(ev_wb) >= 2:
                        waits.append(ev_wb[-2])
                    ev_m = TT(lambda e, ee=ee, b=b: e.matmul(pb[b][:, 0:T],
                                                             selb[:, ee * 128:(ee + 1) * 128],
                                                             wgtT[:, :], start=True, stop=True),
                              waits=waits)
                    ev_e = V(lambda e, ee=ee, b=b: e.tensor_copy(wbc[:, ee, :], pb[b][:, 0:T]),
                             waits=[ev_m])
                    ev_wb.append(ev_e)
                wbc_state["wbc"] = S.cur(sv)

            ev_tst_l = []
            for i, (buf, off) in enumerate([(t1s, OFF_A1), (t3s, OFF_A3)]):
                b = 6 + i % 2
                for kd in range(NKD):
                    w = []
                    if i == 0 and kd == 0:
                        w = [ev_xnT, ev_const_g]
                        if ev_banks67_prev is not None:
                            w.append(ev_banks67_prev)
                    TT(lambda e, kd=kd, off=off, b=b:
                       e.matmul(pb[b][:, 0:T], smb[:, off + kd * 128:off + (kd + 1) * 128],
                                xnT[:, kd, :], start=(kd == 0), stop=(kd == NKD - 1)),
                       waits=w)
                ev_m = S.cur(st)
                ev_e = V(lambda e, buf=buf, b=b: e.tensor_copy(buf[:, :], pb[b][:, 0:T]),
                         waits=[ev_m])
                ev_tst_l.append(ev_e)
            ev_tst = S.cur(sv)
            if debug and g == 0:
                GD(lambda e: e.dma_start(out=dbg["dbg_t1lo"], in_=t1s[:, :]), waits=[ev_tst])

            fdone = {}

            def attn_gen():
                w0 = [ev_xn]
                if ev_outv_prev is not None:
                    w0.append(ev_outv_prev)
                if ev_scrb_guard is not None:
                    w0.append(ev_scrb_guard)
                for c in range(NC_):
                    SC(lambda e, c=c: e.activation(scrA[:, c, :], xn[:, c, :], AF.Identity,
                                                   bias=cst[:, C_QTR:C_QTR + 1], scale=1.0 / C2PI),
                       waits=w0 if c == 0 else ())
                    yield
                for c in range(NC_):
                    SC(lambda e, c=c: e.activation(scrB[:, c, :], scrA[:, c, :], AF.Identity,
                                                   bias=cst[:, C_MAG:C_MAG + 1], scale=1.0))
                    yield
                for c in range(NC_):
                    SC(lambda e, c=c: e.activation(scrB[:, c, :], scrB[:, c, :], AF.Identity,
                                                   bias=cst[:, C_NMAG:C_NMAG + 1], scale=1.0))
                    yield
                ev_st2 = S.cur(ss)
                for c in range(NC_):
                    V(lambda e, c=c: e.scalar_tensor_tensor(scrA[:, c, :], scrB[:, c, :], -C2PI,
                                                            xn[:, c, :], ALU.mult, ALU.add),
                      waits=[ev_st2] if c == 0 else ())
                    yield
                for c in range(NC_):
                    V(lambda e, c=c: e.tensor_scalar(scrA[:, c, :], scrA[:, c, :],
                                                     RR_HI, RR_LO, ALU.min, ALU.max))
                    yield
                ev_rr = S.cur(sv)
                for c in range(NC_):
                    SC(lambda e, c=c: e.activation(scrB[:, c, :], scrA[:, c, :], AF.Sin,
                                                   bias=cst[:, C_PIH:C_PIH + 1], scale=1.0),
                       waits=[ev_rr] if c == 0 else ())
                    yield
                ev_cos = S.cur(ss)
                for c in range(NC_):
                    for h in range(0, 16, 4):
                        for hh2 in range(h, h + 4):
                            V(lambda e, c=c, h=hh2: e.tensor_tensor_scan(
                                scrA[:, c, h * 64:(h + 1) * 64], scrB[:, c, h * 64:(h + 1) * 64],
                                scrB[:, c, h * 64:(h + 1) * 64], 1.0, ALU.mult, ALU.bypass),
                              waits=[ev_cos] if (c == 0 and hh2 == 0) else ())
                        yield
                ev_ao = S.cur(sv)
                if debug and g == 0:
                    OD(lambda e: e.dma_start(out=dbg["dbg_ao"], in_=scrA[:, :, :]), waits=[ev_ao])
                for c in range(NC_):
                    V(lambda e, c=c: e.tensor_add(scrB[:, c, :], xn[:, c, :], scrA[:, c, :]))
                    yield
                for c in range(NC_):
                    V(lambda e, c=c: e.reduce_sum(stats[:, c:c + 1], scrB[:, c, :], axis=AX))
                    yield
                for c in range(NC_):
                    V(lambda e, c=c: e.tensor_scalar(stats[:, 4 + c:5 + c], stats[:, c:c + 1],
                                                     1.0 / D, None, ALU.mult))
                for c in range(NC_):
                    V(lambda e, c=c: e.tensor_scalar(x1[:, c, :], scrB[:, c, :],
                                                     stats[:, 4 + c:5 + c], None, ALU.subtract),
                      waits=[ev_xnT] if c == 0 else ())
                    yield
                ev_c1 = S.cur(sv)
                fdone["x1c"] = ev_c1
                for c in range(NC_):
                    SC(lambda e, c=c: e.activation(scrA[:, c, :], x1[:, c, :], AF.Square),
                       waits=[ev_c1] if c == 0 else ())
                    yield
                ev_sq1 = S.cur(ss)
                for c in range(NC_):
                    V(lambda e, c=c: e.reduce_sum(stats[:, 8 + c:9 + c], scrA[:, c, :], axis=AX),
                      waits=[ev_sq1] if c == 0 else ())
                    yield
                ev_v1 = S.cur(sv)
                for c in range(NC_):
                    SC(lambda e, c=c: e.activation(stats[:, 12 + c:13 + c], stats[:, 8 + c:9 + c],
                                                   AF.Sqrt, bias=cst[:, C_EPS:C_EPS + 1], scale=1.0 / D),
                       waits=[ev_v1] if c == 0 else ())
                    yield
                ev_sd1 = S.cur(ss)
                for c in range(NC_):
                    V(lambda e, c=c: e.reciprocal(stats[:, 12 + c:13 + c], stats[:, 12 + c:13 + c]),
                      waits=[ev_sd1] if c == 0 else ())
                V(lambda e: e.drain(), inc=False)
                yield
                for c in range(NC_):
                    V(lambda e, c=c: e.tensor_scalar(x1[:, c, :], x1[:, c, :],
                                                     stats[:, 12 + c:13 + c], None, ALU.mult))
                    yield
                for c in range(NC_):
                    V(lambda e, c=c: e.tensor_mul(x1[:, c, :], x1[:, c, :], g1b[:, :]))
                    yield
                for c in range(NC_):
                    V(lambda e, c=c: e.tensor_add(x1[:, c, :], x1[:, c, :], be1b[:, :]))
                    yield
                ev_x1 = S.cur(sv)
                fdone["x1"] = ev_x1
                if debug and g == 0:
                    OD(lambda e: e.dma_start(out=dbg["dbg_x1"], in_=x1[:, :, :]), waits=[ev_x1])
                for c in range(NC_):
                    V(lambda e, c=c: e.tensor_scalar(qs1[:, c, :], x1[:, c, 0:NQ],
                                                     1.0 / C2PI, 0.25, ALU.mult, ALU.add),
                      waits=[ev_x1] if c == 0 else ())
                for c in range(NC_):
                    V(lambda e, c=c: e.tensor_scalar(qs2[:, c, :], qs1[:, c, :],
                                                     MAGIC, -MAGIC, ALU.add, ALU.add))
                yield
                for c in range(NC_):
                    V(lambda e, c=c: e.scalar_tensor_tensor(qs1[:, c, :], qs2[:, c, :], -C2PI,
                                                            x1[:, c, 0:NQ], ALU.mult, ALU.add))
                for c in range(NC_):
                    V(lambda e, c=c: e.tensor_scalar(qs1[:, c, :], qs1[:, c, :],
                                                     RR_HI, RR_LO, ALU.min, ALU.max))
                ev_qrr = S.cur(sv)
                yield
                for c in range(NC_):
                    SC(lambda e, c=c: e.activation(qs2[:, c, :], qs1[:, c, :], AF.Sin,
                                                   bias=cst[:, C_PIH:C_PIH + 1], scale=1.0),
                       waits=[ev_qrr] if c == 0 else ())
                ev_qcos = S.cur(ss)
                yield
                for c in range(NC_):
                    V(lambda e, c=c: e.tensor_tensor_scan(qs1[:, c, :], qs2[:, c, :], qs2[:, c, :],
                                                          1.0, ALU.mult, ALU.bypass),
                      waits=[ev_qcos] if c == 0 else ())
                ev_q = S.cur(sv)
                fdone["q"] = ev_q
                if debug and g == 0:
                    OD(lambda e: e.dma_start(out=dbg["dbg_q"], in_=qs1[:, :, :]), waits=[ev_q])

            gen = attn_gen()
            gen_live = True

            def pop_filler():
                nonlocal gen_live
                if gen_live:
                    try:
                        next(gen)
                    except StopIteration:
                        gen_live = False

            NIT = NJF * E
            ev_u1_hist = {}
            ev_u3_hist = {}
            ev_s1_hist = {}
            ev_pr_hist = {}
            ev_sa_hist = {}
            ev_mixmm_hist = {}
            ev_cw1_ev_by_j = {}
            ev_cw3_ev_by_j = {}
            ev_mixev_by_j = {}
            ev_zmm_by_j = {}
            ev_cwdma = {}
            ev_exdma = {}
            ev_cw1mm_by_j = {}
            ev_cw3mm_by_j = {}
            ev_z_last = None
            ev_mix_last = None

            def emit_cw_dma(j):
                w1_ = []
                if j >= 2:
                    w1_.append(ev_cw3mm_by_j[j - 2])
                elif ev_wj_used[j % 2] is not None:
                    w1_.append(ev_wj_used[j % 2])
                if j == 0:
                    w1_.append(ev_const_g)
                ev_cwdma[j] = GD(lambda e, j=j: e.dma_start(out=wjb[:, j % 2, 0:2048],
                                                            in_=wjp[j][:, 0:2048]), waits=w1_)

            def emit_ex_dma(j):
                w2_ = []
                if j >= 2:
                    w2_.append(ev_zmm_by_j[j - 2])
                elif ev_wj_used[j % 2] is not None:
                    w2_.append(ev_wj_used[j % 2])
                ev_exdma[j] = GD(lambda e, j=j: e.dma_start(out=wjb[:, j % 2, 2048:WJ_COLS],
                                                            in_=wjp[j][:, 2048:WJ_COLS]), waits=w2_)

            def emit_cw_mm(j, which, k0):
                bank = j % 2
                off = OFF_WG if which == 1 else OFF_WU
                for kd in range(k0, k0 + 4):
                    w = []
                    if kd == 0:
                        w.append(ev_cwdma[j])
                        if j == 0:
                            w.append(ev_xnT)
                        if which == 1 and j >= 2:
                            w.append(ev_mixev_by_j[j - 2])
                        if which == 3:
                            w.append(ev_cw1_ev_by_j[j])
                    TT(lambda e, kd=kd, j=j, bank=bank, off=off:
                       e.matmul(pb[bank][:, 0:T], wjb[:, j % 2, off + kd * 128:off + (kd + 1) * 128],
                                xnT[:, kd, :], start=(kd == 0), stop=(kd == NKD - 1)),
                       waits=w)
                ev = S.cur(st)
                if which == 1:
                    ev_cw1mm_by_j[j] = ev
                else:
                    ev_cw3mm_by_j[j] = ev

            def emit_cw_evac(j, which):
                mm = ev_cw1mm_by_j[j] if which == 1 else ev_cw3mm_by_j[j]
                buf = cw1s if which == 1 else cw3s
                ev = SC(lambda e, j=j, buf=buf: e.activation(buf[:, j % 2, :], pb[j % 2][:, 0:T],
                                                             AF.Identity),
                        waits=[mm])
                if which == 1:
                    ev_cw1_ev_by_j[j] = ev
                else:
                    ev_cw3_ev_by_j[j] = ev
                if debug and g == 0 and j == 0:
                    nm = "dbg_cw1" if which == 1 else "dbg_cw3"
                    GD(lambda e, buf=buf, nm=nm: e.dma_start(out=dbg[nm], in_=buf[:, 0, :]), waits=[ev])

            emit_cw_dma(0)
            emit_ex_dma(0)
            emit_cw_dma(1)
            emit_ex_dma(1)
            emit_cw_mm(0, 1, 0)
            emit_cw_mm(0, 1, 4)
            emit_cw_evac(0, 1)
            emit_cw_mm(0, 3, 0)
            emit_cw_mm(0, 3, 4)
            emit_cw_evac(0, 3)

            for k in range(NIT + 4):
                in_range = k < NIT
                if in_range:
                    jf, ee = divmod(k, E)
                    par = jf % 2
                    ep = ee % 2
                    qb = 64 * (ee // 4)
                    d1b = 2 + ep
                    d3b = 4 + ep
                    if jf + 1 < NJF:
                        if ee == 4:
                            emit_cw_mm(jf + 1, 1, 0)
                        elif ee == 5:
                            emit_cw_mm(jf + 1, 1, 4)
                        elif ee == 6:
                            emit_cw_evac(jf + 1, 1)
                            emit_cw_mm(jf + 1, 3, 0)
                        elif ee == 7:
                            emit_cw_mm(jf + 1, 3, 4)
                    if ee == 0 and jf >= 1:
                        emit_cw_evac(jf, 3)
                    if ee == 6 and jf + 2 < NJF:
                        emit_cw_dma(jf + 2)
                    if ee == 4 and jf >= 1 and jf + 1 < NJF:
                        emit_ex_dma(jf + 1)
                    w_d1 = []
                    if k == 0:
                        w_d1 = [ev_tst]
                    if k >= 2 and (k - 2) in ev_u1_hist:
                        w_d1 = list(w_d1) + [ev_u1_hist[k - 2]]
                    ev_d1 = TT(lambda e, par=par, ee=ee, qb=qb, b=d1b:
                               e.matmul(pb[b][:, 0:T],
                                        wjb[qb:qb + 64, par, OFF_B1 + 128 * ee:OFF_B1 + 128 * (ee + 1)],
                                        t1s[qb:qb + 64, :], start=True, stop=True),
                               waits=w_d1 + ([ev_exdma[jf]] if ee == 0 else []))
                    ev_u1_hist[(k, "d")] = ev_d1
                    w_d3 = []
                    if k >= 2 and (k - 2) in ev_u3_hist:
                        w_d3 = [ev_u3_hist[k - 2]]
                    ev_d3 = TT(lambda e, par=par, ee=ee, qb=qb, b=d3b:
                               e.matmul(pb[b][:, 0:T],
                                        wjb[qb:qb + 64, par, OFF_B3 + 128 * ee:OFF_B3 + 128 * (ee + 1)],
                                        t3s[qb:qb + 64, :], start=True, stop=True),
                               waits=w_d3)
                    ev_u3_hist[(k, "d")] = ev_d3
                    if k == 0:
                        emit_wgt_t()
                    elif k == 1:
                        emit_wbc()
                m = k - 1
                if 0 <= m < NIT:
                    jfm, eem = divmod(m, E)
                    epm = m % 2
                    w_u1 = [ev_u1_hist[(m, "d")], ev_cw1_ev_by_j[jfm]]
                    if m >= 2 and (m - 2) in ev_pr_hist:
                        w_u1.append(ev_pr_hist[m - 2])
                    ev_u1 = V(lambda e, epm=epm, b=2 + epm, jp=jfm % 2: e.scalar_tensor_tensor(
                        u1b[:, epm, :], pb[b][:, 0:T], 2.0, cw1s[:, jp, :], ALU.mult, ALU.add),
                        waits=w_u1)
                    ev_u1_hist[m] = ev_u1
                    ev_u3 = V(lambda e, epm=epm, b=4 + epm, jp=jfm % 2: e.scalar_tensor_tensor(
                        u3b[:, epm, :], pb[b][:, 0:T], 2.0, cw3s[:, jp, :], ALU.mult, ALU.add),
                        waits=[ev_u3_hist[(m, "d")], ev_cw3_ev_by_j[jfm]])
                    ev_u3_hist[m] = ev_u3
                    w_s1 = [ev_u1]
                    if m >= 2 and (m - 2) in ev_pr_hist:
                        w_s1.append(ev_pr_hist[m - 2])
                    if debug and g == 0 and m == 0:
                        w_s1.append(GD(lambda e: e.dma_start(out=dbg["dbg_u1"], in_=u1b[:, 0, :]),
                                       waits=[ev_u1]))
                    ev_s1 = SC(lambda e, epm=epm: e.activation(u1b[:, epm, :], u1b[:, epm, :], AF.Silu),
                               waits=w_s1)
                    ev_s1_hist[m] = ev_s1
                m = k - 2
                if 0 <= m < NIT:
                    epm = m % 2
                    sp = m % 3
                    w_pr = [ev_s1_hist[m], ev_u3_hist[m]]
                    if m >= 3 and (m - 3) in ev_sa_hist:
                        w_pr.append(ev_sa_hist[m - 3])
                    if m >= 3 and (m - 3) in ev_mixmm_hist:
                        w_pr.append(ev_mixmm_hist[m - 3])
                    ev_pr = V(lambda e, epm=epm, sp=sp: e.tensor_tensor(
                        sab[:, sp, :], u1b[:, epm, :], u3b[:, epm, :], ALU.mult),
                        waits=w_pr)
                    ev_pr_hist[m] = ev_pr
                    w_sa = [ev_pr]
                    if m == 0:
                        w_sa.append(wbc_state["wbc"])
                    ev_sa = GC(lambda e, sp=sp, eem=m % E: e.tensor_tensor(
                        sab[:, sp, :], sab[:, sp, :], wbc[:, eem, :], ALU.mult),
                        waits=w_sa)
                    ev_sa_hist[m] = ev_sa
                    if debug and g == 0 and m == 0:
                        ev_sa_hist[m] = GD(lambda e: e.dma_start(out=dbg["dbg_sact"], in_=sab[:, 0, :]),
                                           waits=[ev_sa])
                m = k - 4
                if 0 <= m < NIT:
                    jfm, eem = divmod(m, E)
                    sp = m % 3
                    lo_m = eem < 4
                    eo_m = eem % 4
                    ev_z = TT(lambda e, eem=eem, eo=eo_m, par=jfm % 2, b=6 if lo_m else 7, sp=sp, jfm=jfm:
                              e.matmul(pb[b][32 * eo:32 * eo + 32, 0:T],
                                       wjb[:, par, OFF_A2 + eem * 32:OFF_A2 + (eem + 1) * 32],
                                       sab[:, sp, :], start=(jfm == 0), stop=(jfm == NJF - 1),
                                       tile_position=(0, 32 * eo), skip_group_check=True),
                              waits=[ev_sa_hist[m]])
                    ev_z_last = ev_z
                    w_mix = []
                    if eem == 0:
                        w_mix = [ev_cw3_ev_by_j[jfm]]
                    ev_mix = TT(lambda e, sp=sp, eem=eem, bank=jfm % 2:
                                e.matmul(pb[bank][:, 0:T], idb[:, :], sab[:, sp, :],
                                         start=(eem == 0), stop=(eem == E - 1)),
                                waits=w_mix)
                    ev_mixmm_hist[m] = ev_mix
                    if eem == E - 1:
                        ev_zmm_by_j[jfm] = ev_z
                        ev_mixev = V(lambda e, jfm=jfm, bank=jfm % 2:
                                     e.tensor_copy(big[:, jfm, :], pb[bank][:, 0:T]),
                                     waits=[ev_mix])
                        ev_mixev_by_j[jfm] = ev_mixev
                        ev_mix_last = ev_mixev
                if k % 2 == 0:
                    pop_filler()
                if g == 0 and conv_state["live"] and k % 12 == 1:
                    try:
                        next(conv_state["gen"])
                    except StopIteration:
                        conv_state["live"] = False
            while gen_live:
                pop_filler()
            ev_wj_used[0] = ev_mix_last
            ev_wj_used[1] = ev_mix_last
            if debug and g == 0:
                GD(lambda e: e.dma_start(out=dbg["dbg_mix"], in_=big[:, :, :]), waits=[ev_mix_last])

            ev_zlo = V(lambda e: e.tensor_scalar(zlos[:, :], pb[6][:, 0:T], 2.0, None, ALU.mult),
                       waits=[ev_z_last])
            ev_zhi = V(lambda e: e.tensor_scalar(zhis[:, :], pb[7][:, 0:T], 2.0, None, ALU.mult),
                       waits=[ev_z_last])
            if debug and g == 0:
                GD(lambda e: e.dma_start(out=dbg["dbg_zlo"], in_=zlos[:, :]), waits=[ev_zlo])
                GD(lambda e: e.dma_start(out=dbg["dbg_zhi"], in_=zhis[:, :]), waits=[ev_zhi])

            ev_mm_wd = {}
            ev_wdbuf = [None] * 4
            for kf in range(NKF):
                parw = kf % 2
                w = []
                if kf >= 2:
                    w.append(ev_mm_wd[kf - 2])
                else:
                    w.append(ev_wsb_guard if ev_wsb_guard is not None else ev_mix_last)
                if g == 0:
                    ev_wdbuf[parw] = GD(lambda e, kf=kf, p=parw:
                                        e.dma_start(out=wsb[:, p, :], in_=wdp[kf]), waits=w)
                else:
                    w.append(CONV_WDW2_DONE)
                    q = YD if kf % 2 == 0 else SD
                    ev_wdbuf[parw] = q(lambda e, kf=kf, p=parw:
                                       e.dma_start(out=wsb[:, p, :], in_=wdq[kf]), waits=w)
                for tc in range(NC_):
                    for hh in range(2):
                        w2_ = []
                        if tc == 0 and hh == 0:
                            w2_.append(ev_wdbuf[parw])
                            if kf == 0:
                                w2_ += [ev_mix_last, ev_zlo, ev_zhi]
                        ev_mm = TT(lambda e, kf=kf, tc=tc, hh=hh, p=parw:
                                   e.matmul(pb[2 * tc + hh][:, 0:T],
                                            big[:, kf, tc * 128:(tc + 1) * 128],
                                            wsb[:, p, hh * 512:(hh + 1) * 512],
                                            start=(kf == 0), stop=False),
                                   waits=w2_)
                ev_mm_wd[kf] = ev_mm
            for tc in range(NC_):
                for hh in range(2):
                    TT(lambda e, tc=tc, hh=hh: e.matmul(pb[2 * tc + hh][:, 0:T],
                                                        zlos[:, tc * 128:(tc + 1) * 128],
                                                        smb[:, OFF_B2LO + hh * 512:OFF_B2LO + (hh + 1) * 512],
                                                        start=False, stop=False),
                       waits=[ev_zlo] if (tc == 0 and hh == 0) else ())
                    TT(lambda e, tc=tc, hh=hh: e.matmul(pb[2 * tc + hh][:, 0:T],
                                                        zhis[:, tc * 128:(tc + 1) * 128],
                                                        smb[:, OFF_B2HI + hh * 512:OFF_B2HI + (hh + 1) * 512],
                                                        start=False, stop=True),
                       waits=[ev_zhi] if (tc == 0 and hh == 0) else ())
            ev_ffnmm = S.cur(st)
            ev_fftap = None
            for tc in range(NC_):
                for hh in range(2):
                    w_si = []
                    if tc == 0 and hh == 0:
                        w_si = [ev_ffnmm, fdone["x1c"]]
                    SC(lambda e, tc=tc, hh=hh: e.activation(scrB[:, tc, hh * 512:(hh + 1) * 512],
                                                            pb[2 * tc + hh][:, 0:T], AF.Silu),
                       waits=w_si)
            ev_g = S.cur(ss)
            if debug and g == 0:
                ev_gtap = OD(lambda e: e.dma_start(out=dbg["dbg_g"], in_=scrB[:, :, :]), waits=[ev_g])
            else:
                ev_gtap = None

            ev_qev = []
            for c in range(NC_):
                b = 2 + c % 2
                waits = [fdone["q"], ev_g] if c == 0 else []
                if len(ev_qev) >= 2:
                    waits.append(ev_qev[-2])
                ev_t = TT(lambda e, c=c, b=b: e.transpose(pb[b][0:16, 0:128], qs1[:, c, :], idf[:, :]),
                          waits=waits)
                ev_e = V(lambda e, c=c, b=b: e.tensor_copy(qT[:, c * 128:(c + 1) * 128], pb[b][0:16, 0:128]),
                         waits=[ev_t])
                ev_qev.append(ev_e)
            ev_qT = S.cur(sv)
            ev_h = []
            for jf in range(NJF):
                b = 2 + jf % 2
                waits = [ev_qT, ev_const_g] if jf == 0 else []
                if len(ev_h) >= 2:
                    waits.append(ev_h[-2])
                ev_m = TT(lambda e, jf=jf, b=b: e.matmul(pb[b][:, 0:T],
                                                         w1b[:, jf * 128:(jf + 1) * 128],
                                                         qT[:, :], start=True, stop=True),
                          waits=waits)
                w_r = [ev_m]
                if jf == 0:
                    w_r.append(ev_ffnmm)
                ev_r = SC(lambda e, jf=jf, b=b: e.activation(big[:, jf, :], pb[b][:, 0:T],
                                                             AF.Relu, bias=b1c[:, jf:jf + 1], scale=1.0),
                          waits=w_r)
                ev_h.append(ev_r)
            ev_rT = S.cur(ss)
            for c in range(NC_):
                V(lambda e, c=c: e.tensor_add(x1[:, c, :], x1[:, c, :], b2bc[:, :]),
                  waits=[ev_qT] if c == 0 else ())
            ev_x1b = S.cur(sv)
            ev_mm_by_kf = {}
            ev_w2buf = [None] * 4
            for kf in range(NKF):
                parw = kf % 2
                waits = []
                if kf >= 2:
                    waits.append(ev_mm_by_kf[kf - 2])
                else:
                    waits.append(ev_mm_wd[NKF - 1])
                if g == 0:
                    ev_w2buf[parw] = GD(lambda e, kf=kf, p=parw:
                                        e.dma_start(out=wsb[:, p, :], in_=w2p[kf]), waits=waits)
                else:
                    waits.append(CONV_WDW2_DONE)
                    q = YD if kf % 2 == 0 else SD
                    ev_w2buf[parw] = q(lambda e, kf=kf, p=parw:
                                       e.dma_start(out=wsb[:, p, :], in_=w2q[kf]), waits=waits)
                for tc in range(NC_):
                    for hh in range(2):
                        w2_ = []
                        if tc == 0 and hh == 0:
                            w2_.append(ev_w2buf[parw])
                            if kf == 0:
                                w2_ += [ev_rT, ev_g]
                        ev_mm = TT(lambda e, kf=kf, tc=tc, hh=hh, p=parw:
                                   e.matmul(pb[2 * tc + hh][:, 0:T],
                                            big[:, kf, tc * 128:(tc + 1) * 128],
                                            wsb[:, p, hh * 512:(hh + 1) * 512],
                                            start=(kf == 0), stop=(kf == NKF - 1)),
                                   waits=w2_)
                ev_mm_by_kf[kf] = ev_mm
            ev_ffmm = S.cur(st)
            ev_wsb_guard = ev_ffmm
            ev_s2 = []
            for c in range(NC_):
                for hh in range(2):
                    w_e = []
                    if c == 0 and hh == 0:
                        w_e = [ev_ffmm, ev_x1b]
                        if ev_fftap is not None:
                            w_e.append(ev_fftap)
                    ev_e = V(lambda e, c=c, hh=hh:
                             e.tensor_tensor(x1[:, c, hh * 512:(hh + 1) * 512],
                                             pb[2 * c + hh][:, 0:T],
                                             x1[:, c, hh * 512:(hh + 1) * 512], ALU.add),
                             waits=w_e)
                    ev_s2.append(ev_e)
            ev_ff = S.cur(sv)
            ev_banks67_prev = ev_ff
            if debug and g == 0:
                OD(lambda e: e.dma_start(out=dbg["dbg_ff"], in_=x1[:, :, :]), waits=[ev_ff])
            for c in range(NC_):
                V(lambda e, c=c: e.scalar_tensor_tensor(sfin[:, c, :], x1[:, c, :], 1.0,
                                                        x1[:, c, :], ALU.mult, ALU.mult,
                                                        accum_out=stats[:, 8 + c:9 + c]))
            for c in range(NC_):
                V(lambda e, c=c: e.reduce_sum(stats[:, c:c + 1], x1[:, c, :], axis=AX))
            for c in range(NC_):
                V(lambda e, c=c: e.tensor_scalar(stats[:, 4 + c:5 + c], stats[:, c:c + 1],
                                                 1.0 / D, None, ALU.mult))
            for c in range(NC_):
                V(lambda e, c=c: e.scalar_tensor_tensor(stats[:, 12 + c:13 + c],
                                                        stats[:, 4 + c:5 + c], -float(D),
                                                        stats[:, 4 + c:5 + c], ALU.mult, ALU.mult))
            for c in range(NC_):
                V(lambda e, c=c: e.tensor_add(stats[:, 8 + c:9 + c], stats[:, 8 + c:9 + c],
                                              stats[:, 12 + c:13 + c]))
            ev_v2 = S.cur(sv)
            for c in range(NC_):
                SC(lambda e, c=c: e.activation(stats[:, 12 + c:13 + c], stats[:, 8 + c:9 + c],
                                               AF.Sqrt, bias=cst[:, C_EPS:C_EPS + 1], scale=1.0 / D),
                   waits=[ev_v2] if c == 0 else ())
            ev_sd2 = S.cur(ss)
            for c in range(NC_):
                V(lambda e, c=c: e.reciprocal(stats[:, 12 + c:13 + c], stats[:, 12 + c:13 + c]),
                  waits=[ev_sd2] if c == 0 else ())
            V(lambda e: e.drain(), inc=False)
            for c in range(NC_):
                V(lambda e, c=c: e.tensor_scalar(sfin[:, c, :], x1[:, c, :],
                                                 stats[:, 4 + c:5 + c], stats[:, 12 + c:13 + c],
                                                 ALU.subtract, ALU.mult))
            for c in range(NC_):
                V(lambda e, c=c: e.tensor_mul(sfin[:, c, :], sfin[:, c, :], g2b[:, :]))
            for c in range(NC_):
                V(lambda e, c=c: e.tensor_add(sfin[:, c, :], sfin[:, c, :], be2b[:, :]))
            ev_s = S.cur(sv)
            if debug and g == 0:
                OD(lambda e: e.dma_start(out=dbg["dbg_s"], in_=sfin[:, :, :]), waits=[ev_s])

            xout = scrA
            for c in range(NC_):
                w_cb = [ev_s, ev_g] if c == 0 else []
                if c == 0 and ev_gtap is not None:
                    w_cb.append(ev_gtap)
                V(lambda e, c=c: e.tensor_tensor(xout[:, c, :], scrB[:, c, :], sfin[:, c, :], ALU.mult),
                  waits=w_cb)
            for c in range(NC_):
                V(lambda e, c=c: e.tensor_tensor(xout[:, c, :], xout[:, c, :], xr[:, c, :], ALU.add))
            ev_outv = S.cur(sv)
            orow = od[g * T:(g + 1) * T, :].rearrange("(p c) d -> p c d", c=NC_)
            ev_odma = OD(lambda e, r=orow: e.dma_start(out=r, in_=xout[:, :, :]), waits=[ev_outv])
            ev_odma_prev = ev_odma
            ev_scrb_guard = ev_gtap
            ev_outv_prev = ev_outv

        S.items["sync"].append((lambda e: None, [S.cur(so), S.cur(sg), S.cur(sqw), S.cur(sx), S.cur(sd)], None))

        with nc.Block() as block:
            @block.sync
            def _(eng):
                for fn, waits, ev in S.items["sync"]:
                    for (sem, val) in waits:
                        eng.wait_ge(sem, val)
                    inst = fn(eng)
                    if ev is not None:
                        inst.then_inc(ev[0], 1)

            @block.gpsimd
            def _(eng):
                S.emit("gpsimd", eng)

            @block.tensor
            def _(eng):
                S.emit("tensor", eng)

            @block.vector
            def _(eng):
                S.emit("vector", eng)

            @block.scalar
            def _(eng):
                S.emit("scalar", eng)

    return nc


def _prep_host(inputs):
    w_gate = np.asarray(inputs["L0_ffn_w_gate"], np.float32)
    w_up = np.asarray(inputs["L0_ffn_w_up"], np.float32)
    w_down = np.asarray(inputs["L0_ffn_w_down"], np.float32)
    w2 = np.asarray(inputs["L0_attn_w2"], np.float32)
    w1 = np.asarray(inputs["L0_attn_w1"], np.float32)
    gate_w = np.asarray(inputs["L0_ffn_gate_w"], np.float32)
    A1 = np.asarray(inputs["L0_ffn_A1"], np.float32)
    B1 = np.asarray(inputs["L0_ffn_B1"], np.float32)
    A3 = np.asarray(inputs["L0_ffn_A3"], np.float32)
    B3 = np.asarray(inputs["L0_ffn_B3"], np.float32)
    A2 = np.asarray(inputs["L0_ffn_A2"], np.float32)
    B2 = np.asarray(inputs["L0_ffn_B2"], np.float32)

    wjp = np.zeros((NJF, 128, WJ_COLS), np.float32)
    wg4 = w_gate.reshape(NJF, 128, NKD, 128).transpose(0, 3, 2, 1)
    wjp[:, :, OFF_WG:OFF_WG + 1024] = wg4.reshape(NJF, 128, NKD * 128)
    wu4 = w_up.reshape(NJF, 128, NKD, 128).transpose(0, 3, 2, 1)
    wjp[:, :, OFF_WU:OFF_WU + 1024] = wu4.reshape(NJF, 128, NKD * 128)
    b1r = B1.reshape(E, NJF, 128, R).transpose(0, 1, 3, 2)
    b3r = B3.reshape(E, NJF, 128, R).transpose(0, 1, 3, 2)
    for ee in range(E):
        rr0 = 64 * (ee // 4) + 16 * (ee % 4)
        wjp[:, rr0:rr0 + 16, OFF_B1 + 128 * ee:OFF_B1 + 128 * (ee + 1)] = b1r[ee]
        wjp[:, rr0:rr0 + 16, OFF_B3 + 128 * ee:OFF_B3 + 128 * (ee + 1)] = b3r[ee]
    a2r = A2.reshape(E, R, NJF, 128).transpose(2, 3, 0, 1)
    a2p = np.zeros((NJF, 128, E, 32), np.float32)
    a2p[:, :, :, :R] = a2r
    wjp[:, :, OFF_A2:OFF_A2 + 256] = a2p.reshape(NJF, 128, 256)

    w2p = np.ascontiguousarray(w2.T.reshape(NKF, 128, 1024))
    wdp = np.ascontiguousarray(w_down.T.reshape(NKF, 128, 1024))

    spk = np.zeros((128, SP_COLS), np.float32)

    def astack(Ax):
        tight = Ax.reshape(E * R, D)
        return tight.T.reshape(NKD, 128, 128).transpose(1, 0, 2).reshape(128, NKD * 128)

    spk[:, OFF_A1:OFF_A1 + 1024] = astack(A1)
    spk[:, OFF_A3:OFF_A3 + 1024] = astack(A3)
    for eo in range(4):
        spk[32 * eo:32 * eo + 16, OFF_B2LO:OFF_B2LO + 1024] = B2[eo].T
        spk[32 * eo:32 * eo + 16, OFF_B2HI:OFF_B2HI + 1024] = B2[4 + eo].T
    gt = gate_w.T.reshape(NKD, 128, 8).transpose(1, 0, 2).reshape(128, NKD * 8)
    spk[:, OFF_GATE:OFF_GATE + 64] = gt

    w1tp = w1.reshape(NJF, 128, R).transpose(2, 0, 1).reshape(16, FF).copy()

    b1c = np.asarray(inputs["L0_attn_b1"], np.float32).reshape(32, 128).T.copy()
    b2c = np.ascontiguousarray(np.broadcast_to(np.asarray(inputs["L0_attn_b2"], np.float32), (128, D)))
    g1b = np.ascontiguousarray(np.broadcast_to(np.asarray(inputs["L0_attn_gamma1"], np.float32), (128, D)))
    be1b = np.ascontiguousarray(np.broadcast_to(np.asarray(inputs["L0_attn_beta1"], np.float32), (128, D)))
    g2b = np.ascontiguousarray(np.broadcast_to(np.asarray(inputs["L0_attn_gamma2"], np.float32), (128, D)))
    be2b = np.ascontiguousarray(np.broadcast_to(np.asarray(inputs["L0_attn_beta2"], np.float32), (128, D)))
    idf = np.eye(128, dtype=np.float32)
    sel = np.zeros((8, 8, 128), np.float32)
    for ee in range(8):
        sel[ee, ee, :] = 1.0
    sel = sel.reshape(8, 1024)
    cst = np.zeros((128, 8), np.float32)
    cst[:, C_EPS] = EPS
    cst[:, C_PIH] = SIN_BIAS
    cst[:, C_QTR] = 0.25
    cst[:, C_MAG] = MAGIC
    cst[:, C_NMAG] = -MAGIC

    return dict(wjp=wjp, w2p=w2p, wdp=wdp, spk=spk, w1t=w1tp, b1d=b1c, b2d=b2c,
                g1d=g1b, be1d=be1b, g2d=g2b, be2d=be2b, idfd=idf,
                seld=sel, cstd=cst, gtd=np.ascontiguousarray(gt))


def kernel(**inputs):
    assert inputs["L0_attn_attn_circuit_params"] is not None
    assert inputs["L0_attn_ffn_circuit_params"] is not None
    x = np.ascontiguousarray(np.asarray(inputs["L0_attn_x"], np.float32).reshape(16384, D))
    shared = _prep_host(inputs)
    nc = build(debug=False)
    in_maps = []
    for i in range(8):
        m = dict(shared)
        xc = x[i * PC:(i + 1) * PC]
        m["x"] = np.ascontiguousarray(
            xc.reshape(G, NC_, 128, D).transpose(0, 2, 1, 3).reshape(PC, D))
        in_maps.append(m)
    res = run_bass_kernel_spmd(nc, in_maps, core_ids=list(range(8)))
    outs = []
    for i in range(8):
        oc = np.asarray(res.results[i]["o"])
        outs.append(oc.reshape(G, 128, NC_, D).transpose(0, 2, 1, 3).reshape(PC, D))
    out = np.concatenate(outs, axis=0)
    return out.reshape(4, 4096, D)


# version 25
# speedup vs baseline: 1.0015x; 1.0015x over previous
import contextlib
import numpy as np
import concourse.bass as bass
import concourse.mybir as mybir
from concourse.bass_utils import run_bass_kernel_spmd

F32 = mybir.dt.float32
BF16 = mybir.dt.bfloat16
AF = mybir.ActivationFunctionType
ALU = mybir.AluOpType
AX = mybir.AxisListType.X

PI = float(np.pi)
C2PI = float(np.float32(2.0 * np.pi))
MAGIC = 3.0 * 4194304.0
SIN_BIAS = float(np.float32(np.pi / 2))
RR_HI = float(np.float32(np.pi - 5e-5 - np.pi / 2))
RR_LO = float(np.float32(-np.pi + 5e-5 - np.pi / 2))
EPS = 1e-5

D = 1024
FF = 4096
E = 8
R = 16
NQ = 16
PC = 2048
T = 512
G = PC // T
NC_ = 4
NKD = D // 128
NJF = FF // 128
NKF = FF // 128

OFF_WG = 0
OFF_WU = 1024
OFF_B1 = 2048
OFF_B3 = 3072
OFF_A2 = 4096
WJ_COLS = 4096 + 256

OFF_A1 = 0
OFF_A3 = 1024
OFF_B2LO = 2048
OFF_B2HI = 3072
OFF_GATE = 4096
SP_COLS = 4096 + 64

C_EPS = 0
C_PIH = 1
C_QTR = 2
C_MAG = 3
C_NMAG = 4


class Sched:
    def __init__(self):
        self.items = {k: [] for k in ("sync", "gpsimd", "tensor", "vector", "scalar")}
        self.cnt = {}

    def ev(self, sem, amt=1):
        self.cnt[sem] = self.cnt.get(sem, 0) + amt
        return (sem, self.cnt[sem])

    def cur(self, sem):
        return (sem, self.cnt.get(sem, 0))

    def op(self, engine, fn, waits=(), inc=None):
        ev = self.ev(inc, 1) if inc is not None else None
        self.items[engine].append((fn, list(waits), ev))
        return ev

    def dma(self, engine, fn, sem, waits=()):
        ev = self.ev(sem, 16)
        self.items[engine].append(
            (lambda e, f=fn, s=sem: f(e).then_inc(s, 16), list(waits), None))
        return ev

    def emit(self, engine, eng):
        for fn, waits, ev in self.items[engine]:
            for (sem, val) in waits:
                eng.wait_ge(sem, val)
            inst = fn(eng)
            if ev is not None:
                inst.then_inc(ev[0], 1)


def build(debug=False):
    nc = bass.Bass("TRN2", target_bir_lowering=False, detect_race_conditions=False)

    xd = nc.dram_tensor("x", [PC, D], F32, kind="ExternalInput").ap()
    wjp = nc.dram_tensor("wjp", [NJF, 128, WJ_COLS], F32, kind="ExternalInput").ap()
    w2p = nc.dram_tensor("w2p", [NKF, 128, 1024], F32, kind="ExternalInput").ap()
    wdp = nc.dram_tensor("wdp", [NKF, 128, 1024], F32, kind="ExternalInput").ap()
    spk = nc.dram_tensor("spk", [128, SP_COLS], F32, kind="ExternalInput").ap()
    w1t = nc.dram_tensor("w1t", [16, FF], F32, kind="ExternalInput").ap()
    b1d = nc.dram_tensor("b1d", [128, 32], F32, kind="ExternalInput").ap()
    b2d = nc.dram_tensor("b2d", [128, D], F32, kind="ExternalInput").ap()
    g1d = nc.dram_tensor("g1d", [128, D], F32, kind="ExternalInput").ap()
    be1d = nc.dram_tensor("be1d", [128, D], F32, kind="ExternalInput").ap()
    g2d = nc.dram_tensor("g2d", [128, D], F32, kind="ExternalInput").ap()
    be2d = nc.dram_tensor("be2d", [128, D], F32, kind="ExternalInput").ap()
    idfd = nc.dram_tensor("idfd", [128, 128], F32, kind="ExternalInput").ap()
    seld = nc.dram_tensor("seld", [8, 1024], F32, kind="ExternalInput").ap()
    gtd = nc.dram_tensor("gtd", [128, 64], F32, kind="ExternalInput").ap()
    cstd = nc.dram_tensor("cstd", [128, 8], F32, kind="ExternalInput").ap()
    od = nc.dram_tensor("o", [PC, D], F32, kind="ExternalOutput").ap()
    wdq = nc.dram_tensor("wdq", [NKF, 128, 1024], BF16, kind="Internal").ap()
    w2q = nc.dram_tensor("w2q", [NKF, 128, 1024], BF16, kind="Internal").ap()

    dbg = {}
    if debug:
        for nm, shp, dt in [
            ("dbg_xn", [128, NC_, D], F32), ("dbg_x1", [128, NC_, D], F32),
            ("dbg_s", [128, NC_, D], F32), ("dbg_probs", [128, NC_, 8], F32),
            ("dbg_wgt", [128, NC_, 8], F32), ("dbg_cw1", [128, T], F32),
            ("dbg_cw3", [128, T], F32), ("dbg_u1", [128, T], F32),
            ("dbg_sact", [128, T], F32), ("dbg_mix", [128, NJF, T], F32),
            ("dbg_zlo", [128, T], F32), ("dbg_zhi", [128, T], F32),
            ("dbg_ff", [128, NKD, T], F32), ("dbg_g", [128, NKD, T], F32),
            ("dbg_ao", [128, NC_, D], F32), ("dbg_q", [128, NC_, NQ], F32),
            ("dbg_t1lo", [128, T], F32), ("dbg_rt", [128, NJF, T], F32),
        ]:
            dbg[nm] = nc.dram_tensor(nm, shp, dt, kind="ExternalOutput").ap()

    S = Sched()
    es = contextlib.ExitStack()
    with es:
        def sb(name, shape, dt):
            return es.enter_context(nc.sbuf_tensor(name, shape, dt))

        cst = sb("cst", [128, 8], F32)
        b1c = sb("b1c", [128, 32], F32)
        b2bc = sb("b2bc", [128, D], F32)
        g1b = sb("g1b", [128, D], F32)
        be1b = sb("be1b", [128, D], F32)
        g2b = sb("g2b", [128, D], F32)
        be2b = sb("be2b", [128, D], F32)
        idf = sb("idf", [128, 128], F32)
        idb = sb("idb", [128, 128], BF16)
        selb = sb("selb", [8, 1024], BF16)
        smb = sb("smb", [128, SP_COLS], BF16)
        w1b = sb("w1b", [16, FF], BF16)

        xr = sb("xr", [128, NC_, D], F32)
        xn = sb("xn", [128, NC_, D], F32)
        sfin = sb("sfin", [128, NC_, D], F32)
        scrA = sb("scrA", [128, NC_, D], F32)
        scrB = sb("scrB", [128, NC_, D], F32)
        xnT = sb("xnT", [128, NKD, T], BF16)
        big = sb("big", [128, NJF, T], BF16)
        stats = sb("stats", [128, 16], F32)
        gprob = sb("gprob", [128, NC_, 8], F32)
        gwc = sb("gwc", [128, NC_, 8], F32)
        mx8c = sb("mx8c", [128, NC_, 8], F32)
        wgt = sb("wgt", [128, NC_, 8], F32)
        wgtT = sb("wgtT", [8, T], BF16)
        wbc = sb("wbc", [128, E, T], BF16)
        qT = sb("qT", [16, T], BF16)
        gtf = sb("gtf", [128, 64], F32)
        xnf = sb("xnf", [128, 3, 128], F32)
        qs1 = sb("qs1", [128, NC_, NQ], F32)
        qs2 = sb("qs2", [128, NC_, NQ], F32)
        cw1s = sb("cw1s", [128, 2, T], BF16)
        cw3s = sb("cw3s", [128, 2, T], BF16)
        u1b = sb("u1b", [128, 2, T], BF16)
        u3b = sb("u3b", [128, 2, T], BF16)
        sab = sb("sab", [128, 3, T], BF16)
        t1s = sb("t1s", [128, T], BF16)
        t3s = sb("t3s", [128, T], BF16)
        zlos = sb("zlos", [128, T], BF16)
        zhis = sb("zhis", [128, T], BF16)
        wjb = sb("wjb", [128, 2, WJ_COLS], BF16)
        wsb = sb("wsb", [128, 2, 1024], BF16)

        pb = [es.enter_context(nc.psum_tensor(f"pb{i}", [128, 512], F32)) for i in range(8)]

        sy = es.enter_context(nc.semaphore("sy"))
        sg = es.enter_context(nc.semaphore("sg"))
        sgc = es.enter_context(nc.semaphore("sgc"))
        st = es.enter_context(nc.semaphore("st"))
        sv = es.enter_context(nc.semaphore("sv"))
        ss = es.enter_context(nc.semaphore("ss"))
        so = es.enter_context(nc.semaphore("so"))
        sqw = es.enter_context(nc.semaphore("sqw"))
        sx = es.enter_context(nc.semaphore("sx"))

        def YD(fn, waits=()):
            return S.dma("sync", fn, sy, waits)

        def OD(fn, waits=()):
            return S.dma("sync", fn, so, waits)

        def GD(fn, waits=()):
            return S.dma("gpsimd", fn, sg, waits)

        def QW(fn, waits=()):
            return S.dma("gpsimd", fn, sqw, waits)

        def XD(fn, waits=()):
            return S.dma("gpsimd", fn, sx, waits)

        def GC(fn, waits=(), inc=True):
            return S.op("gpsimd", fn, waits, sgc if inc else None)

        def TT(fn, waits=(), inc=True):
            return S.op("tensor", fn, waits, st if inc else None)

        def V(fn, waits=(), inc=True):
            return S.op("vector", fn, waits, sv if inc else None)

        def SC(fn, waits=(), inc=True):
            return S.op("scalar", fn, waits, ss if inc else None)

        YD(lambda e: e.dma_start(out=cst[:, :], in_=cstd))
        YD(lambda e: e.dma_start(out=b1c[:, :], in_=b1d))
        YD(lambda e: e.dma_start(out=b2bc[:, :], in_=b2d))
        YD(lambda e: e.dma_start(out=g1b[:, :], in_=g1d))
        YD(lambda e: e.dma_start(out=be1b[:, :], in_=be1d))
        YD(lambda e: e.dma_start(out=g2b[:, :], in_=g2d))
        YD(lambda e: e.dma_start(out=be2b[:, :], in_=be2d))
        YD(lambda e: e.dma_start(out=idf[:, :], in_=idfd))
        YD(lambda e: e.dma_start(out=gtf[:, :], in_=gtd))
        GD(lambda e: e.dma_start(out=idb[:, :], in_=idfd))
        GD(lambda e: e.dma_start(out=selb[:, :], in_=seld))
        GD(lambda e: e.dma_start(out=smb[:, :], in_=spk))
        GD(lambda e: e.dma_start(out=w1b[:, :], in_=w1t))
        ev_const_y = S.cur(sy)
        ev_const_g = S.cur(sg)

        def _conv_gen():
            for ch in range(8):
                QW(lambda e, ch=ch: e.dma_start(
                    out=wdq[4 * ch:4 * ch + 4], in_=wdp[4 * ch:4 * ch + 4]))
                yield
            for ch in range(8):
                QW(lambda e, ch=ch: e.dma_start(
                    out=w2q[4 * ch:4 * ch + 4], in_=w2p[4 * ch:4 * ch + 4]))
                yield

        conv_state = {"gen": _conv_gen(), "live": True}
        CONV_WDW2_DONE = (sqw, 16 * 16)

        ev_zlo_prev = None
        ev_zhi_prev = None
        ev_odma_prev = None
        ev_outv_prev = None
        ev_scrb_guard = None
        ev_banks67_prev = None
        ev_wsb_guard = None
        ev_wj_used = [None, None]

        for g in range(G):
            x1 = xn
            rows = xd[g * T:(g + 1) * T, :].rearrange("(p c) d -> p c d", c=NC_)
            w_x = [] if ev_outv_prev is None else [ev_outv_prev]
            ev_x01 = XD(lambda e, r=rows: e.dma_start(out=xr[:, 0:2, :], in_=r[:, 0:2, :]),
                        waits=w_x)
            ev_x = XD(lambda e, r=rows: e.dma_start(out=xr[:, 2:4, :], in_=r[:, 2:4, :]))
            wait_in = [ev_x01, ev_const_y, ev_const_g]
            wait_in2 = [ev_x]

            for c in range(NC_):
                w_ = wait_in if c == 0 else (wait_in2 if c == 2 else ())
                V(lambda e, c=c: e.scalar_tensor_tensor(xn[:, c, :], xr[:, c, :], 1.0, xr[:, c, :],
                                                        ALU.mult, ALU.mult,
                                                        accum_out=stats[:, 8 + c:9 + c]),
                  waits=w_)
            for c in range(NC_):
                V(lambda e, c=c: e.reduce_sum(stats[:, c:c + 1], xr[:, c, :], axis=AX))
            for c in range(NC_):
                V(lambda e, c=c: e.tensor_scalar(stats[:, 4 + c:5 + c], stats[:, c:c + 1],
                                                 1.0 / D, None, ALU.mult))
            for c in range(NC_):
                V(lambda e, c=c: e.scalar_tensor_tensor(stats[:, 12 + c:13 + c],
                                                        stats[:, 4 + c:5 + c], -float(D),
                                                        stats[:, 4 + c:5 + c], ALU.mult, ALU.mult))
            for c in range(NC_):
                V(lambda e, c=c: e.tensor_add(stats[:, 8 + c:9 + c], stats[:, 8 + c:9 + c],
                                              stats[:, 12 + c:13 + c]))
            ev_var = S.cur(sv)
            for c in range(NC_):
                SC(lambda e, c=c: e.activation(stats[:, 12 + c:13 + c], stats[:, 8 + c:9 + c],
                                               AF.Sqrt, bias=cst[:, C_EPS:C_EPS + 1], scale=1.0 / D),
                   waits=[ev_var] if c == 0 else ())
            ev_sd = S.cur(ss)
            for c in range(NC_):
                V(lambda e, c=c: e.reciprocal(stats[:, 12 + c:13 + c], stats[:, 12 + c:13 + c]),
                  waits=[ev_sd] if c == 0 else ())
            V(lambda e: e.drain(), inc=False)
            ev_xn_c = []
            for c in range(NC_):
                ev_xn_c.append(V(lambda e, c=c: e.tensor_scalar(
                    xn[:, c, :], xr[:, c, :],
                    stats[:, 4 + c:5 + c], stats[:, 12 + c:13 + c],
                    ALU.subtract, ALU.mult)))
            ev_xn = S.cur(sv)
            if debug and g == 0:
                OD(lambda e: e.dma_start(out=dbg["dbg_xn"], in_=xn[:, :, :]), waits=[ev_xn])

            ev_tr = []
            ev_gcopy = []
            ev_gm_hist = []
            ev_f_hist = []

            def emit_gate_mm(i2):
                c2, kd2 = divmod(i2, NKD)
                gb2 = 4 + c2 % 2
                w_g = [ev_f_hist[i2]]
                if kd2 == 0 and c2 >= 2:
                    w_g.append(ev_gcopy[c2 - 2])
                ev_gm = TT(lambda e, kd=kd2, gb=gb2, par=i2 % 3:
                           e.matmul(pb[gb][0:128, 0:8], xnf[:, par, :],
                                    gtf[:, kd * 8:(kd + 1) * 8],
                                    start=(kd == 0), stop=(kd == NKD - 1)),
                           waits=w_g)
                ev_gm_hist.append(ev_gm)
                if kd2 == NKD - 1:
                    ev_cp = V(lambda e, c=c2, gb=gb2:
                              e.tensor_copy(gprob[:, c, :], pb[gb][0:128, 0:8]),
                              waits=[S.cur(st)])
                    ev_gcopy.append(ev_cp)

            for c in range(NC_):
                for kd in range(NKD):
                    i = c * NKD + kd
                    b = (2, 3, 6, 7)[i % 4]
                    par = i % 3
                    waits = [ev_xn_c[c]] if kd == 0 else []
                    if i in (2, 3) and ev_banks67_prev is not None:
                        waits.append(ev_banks67_prev)
                    if len(ev_tr) >= 4:
                        waits.append(ev_tr[-4][1])
                    ev_t = TT(lambda e, c=c, kd=kd, b=b:
                              e.transpose(pb[b][0:128, 0:128],
                                          xn[:, c, kd * 128:(kd + 1) * 128], idf[:, :]),
                              waits=waits)
                    ev_e = V(lambda e, c=c, kd=kd, b=b:
                             e.tensor_copy(xnT[:, kd, c * 128:(c + 1) * 128], pb[b][0:128, 0:128]),
                             waits=[ev_t])
                    w_f = []
                    if i >= 3:
                        w_f.append(ev_gm_hist[i - 3])
                    ev_f = V(lambda e, b=b, par=par:
                             e.tensor_copy(xnf[:, par, :], pb[b][0:128, 0:128]),
                             waits=w_f)
                    ev_f_hist.append(ev_f)
                    ev_tr.append((ev_e, ev_f))
                    if i >= 2:
                        emit_gate_mm(i - 2)
            emit_gate_mm(30)
            emit_gate_mm(31)
            ev_xnT = S.cur(sv)

            ev_glog = ev_gcopy[-1]
            for c in range(NC_):
                V(lambda e, c=c: e.reduce_max(stats[:, c:c + 1], gprob[:, c, :], axis=AX),
                  waits=[ev_glog] if c == 0 else ())
            V(lambda e: e.drain(), inc=False)
            for c in range(NC_):
                V(lambda e, c=c: e.tensor_scalar(stats[:, 4 + c:5 + c], stats[:, c:c + 1],
                                                 -1.0, None, ALU.mult))
            ev_gmax = S.cur(sv)
            for c in range(NC_):
                SC(lambda e, c=c: e.activation(gprob[:, c, :], gprob[:, c, :], AF.Exp,
                                               bias=stats[:, 4 + c:5 + c], scale=1.0),
                   waits=[ev_gmax] if c == 0 else ())
            ev_gexp = S.cur(ss)
            for c in range(NC_):
                V(lambda e, c=c: e.reduce_sum(stats[:, 8 + c:9 + c], gprob[:, c, :], axis=AX),
                  waits=[ev_gexp] if c == 0 else ())
            V(lambda e: e.drain(), inc=False)
            for c in range(NC_):
                V(lambda e, c=c: e.reciprocal(stats[:, 12 + c:13 + c], stats[:, 8 + c:9 + c]))
            V(lambda e: e.drain(), inc=False)
            for c in range(NC_):
                V(lambda e, c=c: e.tensor_scalar(gprob[:, c, :], gprob[:, c, :],
                                                 stats[:, 12 + c:13 + c], None, ALU.mult))
            V(lambda e: e.drain(), inc=False)
            ev_ptap = None
            if debug and g == 0:
                ev_p_ = S.cur(sv)
                ev_ptap = OD(lambda e: e.dma_start(out=dbg["dbg_probs"], in_=gprob[:, :, :]), waits=[ev_p_])
            for c in range(NC_):
                V(lambda e, c=c: e.max(out=mx8c[:, c, :], in_=gprob[:, c, :]),
                  waits=[ev_ptap] if (c == 0 and ev_ptap is not None) else ())
            V(lambda e: e.drain(), inc=False)
            for c in range(NC_):
                V(lambda e, c=c: e.tensor_add(stats[:, c:c + 1], mx8c[:, c, 0:1], mx8c[:, c, 1:2]))
            for c in range(NC_):
                V(lambda e, c=c: e.tensor_scalar(gwc[:, c, :], gprob[:, c, :], mx8c[:, c, 1:2],
                                                 -1e30, ALU.subtract, ALU.mult))
            V(lambda e: e.drain(), inc=False)
            for c in range(NC_):
                V(lambda e, c=c: e.reciprocal(stats[:, 4 + c:5 + c], stats[:, c:c + 1]))
            for c in range(NC_):
                V(lambda e, c=c: e.tensor_scalar(gwc[:, c, :], gwc[:, c, :], 1.0, 0.0, ALU.min, ALU.max))
            V(lambda e: e.drain(), inc=False)
            for c in range(NC_):
                V(lambda e, c=c: e.tensor_scalar(gwc[:, c, :], gwc[:, c, :], -1.0, 1.0, ALU.mult, ALU.add))
            for c in range(NC_):
                V(lambda e, c=c: e.tensor_scalar(gprob[:, c, :], gprob[:, c, :],
                                                 stats[:, 4 + c:5 + c], None, ALU.mult))
            V(lambda e: e.drain(), inc=False)
            for c in range(NC_):
                V(lambda e, c=c: e.tensor_mul(wgt[:, c, :], gprob[:, c, :], gwc[:, c, :]))
            V(lambda e: e.drain(), inc=False)
            ev_wgt = S.cur(sv)
            if debug and g == 0:
                OD(lambda e: e.dma_start(out=dbg["dbg_wgt"], in_=wgt[:, :, :]), waits=[ev_wgt])

            wbc_state = {}

            def emit_wgt_t():
                ev_wt = []
                for c in range(NC_):
                    b = 6 + c % 2
                    waits = [ev_wgt, ev_tst] if c == 0 else []
                    if len(ev_wt) >= 2:
                        waits.append(ev_wt[-2])
                    ev_t = TT(lambda e, c=c, b=b: e.transpose(pb[b][0:8, 0:128],
                                                              wgt[:, c, :], idf[:, :]),
                              waits=waits)
                    ev_e = V(lambda e, c=c, b=b: e.tensor_copy(wgtT[:, c * 128:(c + 1) * 128],
                                                               pb[b][0:8, 0:128]),
                             waits=[ev_t])
                    ev_wt.append(ev_e)
                wbc_state["wgtT"] = S.cur(sv)

            def emit_wbc():
                ev_wb = []
                for ee in range(E):
                    b = 6 + ee % 2
                    waits = [wbc_state["wgtT"]] if ee == 0 else []
                    if len(ev_wb) >= 2:
                        waits.append(ev_wb[-2])
                    ev_m = TT(lambda e, ee=ee, b=b: e.matmul(pb[b][:, 0:T],
                                                             selb[:, ee * 128:(ee + 1) * 128],
                                                             wgtT[:, :], start=True, stop=True),
                              waits=waits)
                    ev_e = V(lambda e, ee=ee, b=b: e.tensor_copy(wbc[:, ee, :], pb[b][:, 0:T]),
                             waits=[ev_m])
                    ev_wb.append(ev_e)
                wbc_state["wbc"] = S.cur(sv)

            ev_tst_l = []
            for i, (buf, off) in enumerate([(t1s, OFF_A1), (t3s, OFF_A3)]):
                b = 6 + i % 2
                for kd in range(NKD):
                    w = []
                    if i == 0 and kd == 0:
                        w = [ev_xnT, ev_const_g]
                        if ev_banks67_prev is not None:
                            w.append(ev_banks67_prev)
                    TT(lambda e, kd=kd, off=off, b=b:
                       e.matmul(pb[b][:, 0:T], smb[:, off + kd * 128:off + (kd + 1) * 128],
                                xnT[:, kd, :], start=(kd == 0), stop=(kd == NKD - 1)),
                       waits=w)
                ev_m = S.cur(st)
                ev_e = V(lambda e, buf=buf, b=b: e.tensor_copy(buf[:, :], pb[b][:, 0:T]),
                         waits=[ev_m])
                ev_tst_l.append(ev_e)
            ev_tst = S.cur(sv)
            if debug and g == 0:
                GD(lambda e: e.dma_start(out=dbg["dbg_t1lo"], in_=t1s[:, :]), waits=[ev_tst])

            fdone = {}

            def attn_gen():
                w0 = [ev_xn]
                if ev_outv_prev is not None:
                    w0.append(ev_outv_prev)
                if ev_scrb_guard is not None:
                    w0.append(ev_scrb_guard)
                for c in range(NC_):
                    SC(lambda e, c=c: e.activation(scrA[:, c, :], xn[:, c, :], AF.Identity,
                                                   bias=cst[:, C_QTR:C_QTR + 1], scale=1.0 / C2PI),
                       waits=w0 if c == 0 else ())
                    yield
                for c in range(NC_):
                    SC(lambda e, c=c: e.activation(scrB[:, c, :], scrA[:, c, :], AF.Identity,
                                                   bias=cst[:, C_MAG:C_MAG + 1], scale=1.0))
                    yield
                for c in range(NC_):
                    SC(lambda e, c=c: e.activation(scrB[:, c, :], scrB[:, c, :], AF.Identity,
                                                   bias=cst[:, C_NMAG:C_NMAG + 1], scale=1.0))
                    yield
                ev_st2 = S.cur(ss)
                for c in range(NC_):
                    V(lambda e, c=c: e.scalar_tensor_tensor(scrA[:, c, :], scrB[:, c, :], -C2PI,
                                                            xn[:, c, :], ALU.mult, ALU.add),
                      waits=[ev_st2] if c == 0 else ())
                    yield
                for c in range(NC_):
                    V(lambda e, c=c: e.tensor_scalar(scrA[:, c, :], scrA[:, c, :],
                                                     RR_HI, RR_LO, ALU.min, ALU.max))
                    yield
                ev_rr = S.cur(sv)
                for c in range(NC_):
                    SC(lambda e, c=c: e.activation(scrB[:, c, :], scrA[:, c, :], AF.Sin,
                                                   bias=cst[:, C_PIH:C_PIH + 1], scale=1.0),
                       waits=[ev_rr] if c == 0 else ())
                    yield
                ev_cos = S.cur(ss)
                for c in range(NC_):
                    for h in range(0, 16, 4):
                        for hh2 in range(h, h + 4):
                            V(lambda e, c=c, h=hh2: e.tensor_tensor_scan(
                                scrA[:, c, h * 64:(h + 1) * 64], scrB[:, c, h * 64:(h + 1) * 64],
                                scrB[:, c, h * 64:(h + 1) * 64], 1.0, ALU.mult, ALU.bypass),
                              waits=[ev_cos] if (c == 0 and hh2 == 0) else ())
                        yield
                ev_ao = S.cur(sv)
                if debug and g == 0:
                    OD(lambda e: e.dma_start(out=dbg["dbg_ao"], in_=scrA[:, :, :]), waits=[ev_ao])
                for c in range(NC_):
                    V(lambda e, c=c: e.tensor_add(scrB[:, c, :], xn[:, c, :], scrA[:, c, :]))
                    yield
                for c in range(NC_):
                    V(lambda e, c=c: e.reduce_sum(stats[:, c:c + 1], scrB[:, c, :], axis=AX))
                    yield
                for c in range(NC_):
                    V(lambda e, c=c: e.tensor_scalar(stats[:, 4 + c:5 + c], stats[:, c:c + 1],
                                                     1.0 / D, None, ALU.mult))
                for c in range(NC_):
                    V(lambda e, c=c: e.tensor_scalar(x1[:, c, :], scrB[:, c, :],
                                                     stats[:, 4 + c:5 + c], None, ALU.subtract),
                      waits=[ev_xnT] if c == 0 else ())
                    yield
                ev_c1 = S.cur(sv)
                fdone["x1c"] = ev_c1
                for c in range(NC_):
                    SC(lambda e, c=c: e.activation(scrA[:, c, :], x1[:, c, :], AF.Square),
                       waits=[ev_c1] if c == 0 else ())
                    yield
                ev_sq1 = S.cur(ss)
                for c in range(NC_):
                    V(lambda e, c=c: e.reduce_sum(stats[:, 8 + c:9 + c], scrA[:, c, :], axis=AX),
                      waits=[ev_sq1] if c == 0 else ())
                    yield
                ev_v1 = S.cur(sv)
                for c in range(NC_):
                    SC(lambda e, c=c: e.activation(stats[:, 12 + c:13 + c], stats[:, 8 + c:9 + c],
                                                   AF.Sqrt, bias=cst[:, C_EPS:C_EPS + 1], scale=1.0 / D),
                       waits=[ev_v1] if c == 0 else ())
                    yield
                ev_sd1 = S.cur(ss)
                for c in range(NC_):
                    V(lambda e, c=c: e.reciprocal(stats[:, 12 + c:13 + c], stats[:, 12 + c:13 + c]),
                      waits=[ev_sd1] if c == 0 else ())
                V(lambda e: e.drain(), inc=False)
                yield
                for c in range(NC_):
                    V(lambda e, c=c: e.tensor_scalar(x1[:, c, :], x1[:, c, :],
                                                     stats[:, 12 + c:13 + c], None, ALU.mult))
                    yield
                for c in range(NC_):
                    V(lambda e, c=c: e.tensor_mul(x1[:, c, :], x1[:, c, :], g1b[:, :]))
                    yield
                for c in range(NC_):
                    V(lambda e, c=c: e.tensor_add(x1[:, c, :], x1[:, c, :], be1b[:, :]))
                    yield
                ev_x1 = S.cur(sv)
                fdone["x1"] = ev_x1
                if debug and g == 0:
                    OD(lambda e: e.dma_start(out=dbg["dbg_x1"], in_=x1[:, :, :]), waits=[ev_x1])
                for c in range(NC_):
                    V(lambda e, c=c: e.tensor_scalar(qs1[:, c, :], x1[:, c, 0:NQ],
                                                     1.0 / C2PI, 0.25, ALU.mult, ALU.add),
                      waits=[ev_x1] if c == 0 else ())
                for c in range(NC_):
                    V(lambda e, c=c: e.tensor_scalar(qs2[:, c, :], qs1[:, c, :],
                                                     MAGIC, -MAGIC, ALU.add, ALU.add))
                yield
                for c in range(NC_):
                    V(lambda e, c=c: e.scalar_tensor_tensor(qs1[:, c, :], qs2[:, c, :], -C2PI,
                                                            x1[:, c, 0:NQ], ALU.mult, ALU.add))
                for c in range(NC_):
                    V(lambda e, c=c: e.tensor_scalar(qs1[:, c, :], qs1[:, c, :],
                                                     RR_HI, RR_LO, ALU.min, ALU.max))
                ev_qrr = S.cur(sv)
                yield
                for c in range(NC_):
                    SC(lambda e, c=c: e.activation(qs2[:, c, :], qs1[:, c, :], AF.Sin,
                                                   bias=cst[:, C_PIH:C_PIH + 1], scale=1.0),
                       waits=[ev_qrr] if c == 0 else ())
                ev_qcos = S.cur(ss)
                yield
                for c in range(NC_):
                    V(lambda e, c=c: e.tensor_tensor_scan(qs1[:, c, :], qs2[:, c, :], qs2[:, c, :],
                                                          1.0, ALU.mult, ALU.bypass),
                      waits=[ev_qcos] if c == 0 else ())
                ev_q = S.cur(sv)
                fdone["q"] = ev_q
                if debug and g == 0:
                    OD(lambda e: e.dma_start(out=dbg["dbg_q"], in_=qs1[:, :, :]), waits=[ev_q])

            gen = attn_gen()
            gen_live = True

            def pop_filler():
                nonlocal gen_live
                if gen_live:
                    try:
                        next(gen)
                    except StopIteration:
                        gen_live = False

            NIT = NJF * E
            ev_u1_hist = {}
            ev_u3_hist = {}
            ev_s1_hist = {}
            ev_pr_hist = {}
            ev_sa_hist = {}
            ev_mixmm_hist = {}
            ev_cw1_ev_by_j = {}
            ev_cw3_ev_by_j = {}
            ev_mixev_by_j = {}
            ev_zmm_by_j = {}
            ev_cwdma = {}
            ev_exdma = {}
            ev_cw1mm_by_j = {}
            ev_cw3mm_by_j = {}
            ev_z_last = None
            ev_mix_last = None

            def emit_cw_dma(j):
                w1_ = []
                if j >= 2:
                    w1_.append(ev_cw3mm_by_j[j - 2])
                elif ev_wj_used[j % 2] is not None:
                    w1_.append(ev_wj_used[j % 2])
                if j == 0:
                    w1_.append(ev_const_g)
                ev_cwdma[j] = GD(lambda e, j=j: e.dma_start(out=wjb[:, j % 2, 0:2048],
                                                            in_=wjp[j][:, 0:2048]), waits=w1_)

            def emit_ex_dma(j):
                w2_ = []
                if j >= 2:
                    w2_.append(ev_zmm_by_j[j - 2])
                elif ev_wj_used[j % 2] is not None:
                    w2_.append(ev_wj_used[j % 2])
                ev_exdma[j] = GD(lambda e, j=j: e.dma_start(out=wjb[:, j % 2, 2048:WJ_COLS],
                                                            in_=wjp[j][:, 2048:WJ_COLS]), waits=w2_)

            def emit_cw_mm(j, which, k0):
                bank = j % 2
                off = OFF_WG if which == 1 else OFF_WU
                for kd in range(k0, k0 + 4):
                    w = []
                    if kd == 0:
                        w.append(ev_cwdma[j])
                        if j == 0:
                            w.append(ev_xnT)
                        if which == 1 and j >= 2:
                            w.append(ev_mixev_by_j[j - 2])
                        if which == 3:
                            w.append(ev_cw1_ev_by_j[j])
                    TT(lambda e, kd=kd, j=j, bank=bank, off=off:
                       e.matmul(pb[bank][:, 0:T], wjb[:, j % 2, off + kd * 128:off + (kd + 1) * 128],
                                xnT[:, kd, :], start=(kd == 0), stop=(kd == NKD - 1)),
                       waits=w)
                ev = S.cur(st)
                if which == 1:
                    ev_cw1mm_by_j[j] = ev
                else:
                    ev_cw3mm_by_j[j] = ev

            def emit_cw_evac(j, which):
                mm = ev_cw1mm_by_j[j] if which == 1 else ev_cw3mm_by_j[j]
                buf = cw1s if which == 1 else cw3s
                ev = SC(lambda e, j=j, buf=buf: e.activation(buf[:, j % 2, :], pb[j % 2][:, 0:T],
                                                             AF.Identity),
                        waits=[mm])
                if which == 1:
                    ev_cw1_ev_by_j[j] = ev
                else:
                    ev_cw3_ev_by_j[j] = ev
                if debug and g == 0 and j == 0:
                    nm = "dbg_cw1" if which == 1 else "dbg_cw3"
                    GD(lambda e, buf=buf, nm=nm: e.dma_start(out=dbg[nm], in_=buf[:, 0, :]), waits=[ev])

            emit_cw_dma(0)
            emit_ex_dma(0)
            emit_cw_dma(1)
            emit_ex_dma(1)
            emit_cw_mm(0, 1, 0)
            emit_cw_mm(0, 1, 4)
            emit_cw_evac(0, 1)
            emit_cw_mm(0, 3, 0)
            emit_cw_mm(0, 3, 4)
            emit_cw_evac(0, 3)

            for k in range(NIT + 4):
                in_range = k < NIT
                if in_range:
                    jf, ee = divmod(k, E)
                    par = jf % 2
                    ep = ee % 2
                    qb = 64 * (ee // 4)
                    d1b = 2 + ep
                    d3b = 4 + ep
                    if jf + 1 < NJF:
                        if ee == 4:
                            emit_cw_mm(jf + 1, 1, 0)
                        elif ee == 5:
                            emit_cw_mm(jf + 1, 1, 4)
                        elif ee == 6:
                            emit_cw_evac(jf + 1, 1)
                            emit_cw_mm(jf + 1, 3, 0)
                        elif ee == 7:
                            emit_cw_mm(jf + 1, 3, 4)
                    if ee == 0 and jf >= 1:
                        emit_cw_evac(jf, 3)
                    if ee == 6 and jf + 2 < NJF:
                        emit_cw_dma(jf + 2)
                    if ee == 4 and jf >= 1 and jf + 1 < NJF:
                        emit_ex_dma(jf + 1)
                    w_d1 = []
                    if k == 0:
                        w_d1 = [ev_tst]
                    if k >= 2 and (k - 2) in ev_u1_hist:
                        w_d1 = list(w_d1) + [ev_u1_hist[k - 2]]
                    ev_d1 = TT(lambda e, par=par, ee=ee, qb=qb, b=d1b:
                               e.matmul(pb[b][:, 0:T],
                                        wjb[qb:qb + 64, par, OFF_B1 + 128 * ee:OFF_B1 + 128 * (ee + 1)],
                                        t1s[qb:qb + 64, :], start=True, stop=True),
                               waits=w_d1 + ([ev_exdma[jf]] if ee == 0 else []))
                    ev_u1_hist[(k, "d")] = ev_d1
                    w_d3 = []
                    if k >= 2 and (k - 2) in ev_u3_hist:
                        w_d3 = [ev_u3_hist[k - 2]]
                    ev_d3 = TT(lambda e, par=par, ee=ee, qb=qb, b=d3b:
                               e.matmul(pb[b][:, 0:T],
                                        wjb[qb:qb + 64, par, OFF_B3 + 128 * ee:OFF_B3 + 128 * (ee + 1)],
                                        t3s[qb:qb + 64, :], start=True, stop=True),
                               waits=w_d3)
                    ev_u3_hist[(k, "d")] = ev_d3
                    if k == 0:
                        emit_wgt_t()
                    elif k == 1:
                        emit_wbc()
                m = k - 1
                if 0 <= m < NIT:
                    jfm, eem = divmod(m, E)
                    epm = m % 2
                    w_u1 = [ev_u1_hist[(m, "d")], ev_cw1_ev_by_j[jfm]]
                    if m >= 2 and (m - 2) in ev_pr_hist:
                        w_u1.append(ev_pr_hist[m - 2])
                    ev_u1 = V(lambda e, epm=epm, b=2 + epm, jp=jfm % 2: e.scalar_tensor_tensor(
                        u1b[:, epm, :], pb[b][:, 0:T], 2.0, cw1s[:, jp, :], ALU.mult, ALU.add),
                        waits=w_u1)
                    ev_u1_hist[m] = ev_u1
                    ev_u3 = V(lambda e, epm=epm, b=4 + epm, jp=jfm % 2: e.scalar_tensor_tensor(
                        u3b[:, epm, :], pb[b][:, 0:T], 2.0, cw3s[:, jp, :], ALU.mult, ALU.add),
                        waits=[ev_u3_hist[(m, "d")], ev_cw3_ev_by_j[jfm]])
                    ev_u3_hist[m] = ev_u3
                    w_s1 = [ev_u1]
                    if m >= 2 and (m - 2) in ev_pr_hist:
                        w_s1.append(ev_pr_hist[m - 2])
                    if debug and g == 0 and m == 0:
                        w_s1.append(GD(lambda e: e.dma_start(out=dbg["dbg_u1"], in_=u1b[:, 0, :]),
                                       waits=[ev_u1]))
                    ev_s1 = SC(lambda e, epm=epm: e.activation(u1b[:, epm, :], u1b[:, epm, :], AF.Silu),
                               waits=w_s1)
                    ev_s1_hist[m] = ev_s1
                m = k - 2
                if 0 <= m < NIT:
                    epm = m % 2
                    sp = m % 3
                    w_pr = [ev_s1_hist[m], ev_u3_hist[m]]
                    if m >= 3 and (m - 3) in ev_sa_hist:
                        w_pr.append(ev_sa_hist[m - 3])
                    if m >= 3 and (m - 3) in ev_mixmm_hist:
                        w_pr.append(ev_mixmm_hist[m - 3])
                    ev_pr = V(lambda e, epm=epm, sp=sp: e.tensor_tensor(
                        sab[:, sp, :], u1b[:, epm, :], u3b[:, epm, :], ALU.mult),
                        waits=w_pr)
                    ev_pr_hist[m] = ev_pr
                    w_sa = [ev_pr]
                    if m == 0:
                        w_sa.append(wbc_state["wbc"])
                    ev_sa = GC(lambda e, sp=sp, eem=m % E: e.tensor_tensor(
                        sab[:, sp, :], sab[:, sp, :], wbc[:, eem, :], ALU.mult),
                        waits=w_sa)
                    ev_sa_hist[m] = ev_sa
                    if debug and g == 0 and m == 0:
                        ev_sa_hist[m] = GD(lambda e: e.dma_start(out=dbg["dbg_sact"], in_=sab[:, 0, :]),
                                           waits=[ev_sa])
                m = k - 4
                if 0 <= m < NIT:
                    jfm, eem = divmod(m, E)
                    sp = m % 3
                    lo_m = eem < 4
                    eo_m = eem % 4
                    ev_z = TT(lambda e, eem=eem, eo=eo_m, par=jfm % 2, b=6 if lo_m else 7, sp=sp, jfm=jfm:
                              e.matmul(pb[b][32 * eo:32 * eo + 32, 0:T],
                                       wjb[:, par, OFF_A2 + eem * 32:OFF_A2 + (eem + 1) * 32],
                                       sab[:, sp, :], start=(jfm == 0), stop=(jfm == NJF - 1),
                                       tile_position=(0, 32 * eo), skip_group_check=True),
                              waits=[ev_sa_hist[m]])
                    ev_z_last = ev_z
                    w_mix = []
                    if eem == 0:
                        w_mix = [ev_cw3_ev_by_j[jfm]]
                    ev_mix = TT(lambda e, sp=sp, eem=eem, bank=jfm % 2:
                                e.matmul(pb[bank][:, 0:T], idb[:, :], sab[:, sp, :],
                                         start=(eem == 0), stop=(eem == E - 1)),
                                waits=w_mix)
                    ev_mixmm_hist[m] = ev_mix
                    if eem == E - 1:
                        ev_zmm_by_j[jfm] = ev_z
                        ev_mixev = V(lambda e, jfm=jfm, bank=jfm % 2:
                                     e.tensor_copy(big[:, jfm, :], pb[bank][:, 0:T]),
                                     waits=[ev_mix])
                        ev_mixev_by_j[jfm] = ev_mixev
                        ev_mix_last = ev_mixev
                if k % 2 == 0:
                    pop_filler()
                if g == 0 and conv_state["live"] and k % 12 == 1:
                    try:
                        next(conv_state["gen"])
                    except StopIteration:
                        conv_state["live"] = False
            while gen_live:
                pop_filler()
            ev_wj_used[0] = ev_mix_last
            ev_wj_used[1] = ev_mix_last
            if debug and g == 0:
                GD(lambda e: e.dma_start(out=dbg["dbg_mix"], in_=big[:, :, :]), waits=[ev_mix_last])

            ev_zlo = V(lambda e: e.tensor_scalar(zlos[:, :], pb[6][:, 0:T], 2.0, None, ALU.mult),
                       waits=[ev_z_last])
            ev_zhi = V(lambda e: e.tensor_scalar(zhis[:, :], pb[7][:, 0:T], 2.0, None, ALU.mult),
                       waits=[ev_z_last])
            if debug and g == 0:
                GD(lambda e: e.dma_start(out=dbg["dbg_zlo"], in_=zlos[:, :]), waits=[ev_zlo])
                GD(lambda e: e.dma_start(out=dbg["dbg_zhi"], in_=zhis[:, :]), waits=[ev_zhi])

            ev_mm_wd = {}
            ev_wdbuf = [None] * 4
            for kf in range(NKF):
                parw = kf % 2
                w = []
                if kf >= 2:
                    w.append(ev_mm_wd[kf - 2])
                else:
                    w.append(ev_wsb_guard if ev_wsb_guard is not None else ev_mix_last)
                if g == 0:
                    ev_wdbuf[parw] = GD(lambda e, kf=kf, p=parw:
                                        e.dma_start(out=wsb[:, p, :], in_=wdp[kf]), waits=w)
                else:
                    w.append(CONV_WDW2_DONE)
                    ev_wdbuf[parw] = YD(lambda e, kf=kf, p=parw:
                                        e.dma_start(out=wsb[:, p, :], in_=wdq[kf]), waits=w)
                for tc in range(NC_):
                    for hh in range(2):
                        w2_ = []
                        if tc == 0 and hh == 0:
                            w2_.append(ev_wdbuf[parw])
                            if kf == 0:
                                w2_ += [ev_mix_last, ev_zlo, ev_zhi]
                        ev_mm = TT(lambda e, kf=kf, tc=tc, hh=hh, p=parw:
                                   e.matmul(pb[2 * tc + hh][:, 0:T],
                                            big[:, kf, tc * 128:(tc + 1) * 128],
                                            wsb[:, p, hh * 512:(hh + 1) * 512],
                                            start=(kf == 0), stop=False),
                                   waits=w2_)
                ev_mm_wd[kf] = ev_mm
            for tc in range(NC_):
                for hh in range(2):
                    TT(lambda e, tc=tc, hh=hh: e.matmul(pb[2 * tc + hh][:, 0:T],
                                                        zlos[:, tc * 128:(tc + 1) * 128],
                                                        smb[:, OFF_B2LO + hh * 512:OFF_B2LO + (hh + 1) * 512],
                                                        start=False, stop=False),
                       waits=[ev_zlo] if (tc == 0 and hh == 0) else ())
                    TT(lambda e, tc=tc, hh=hh: e.matmul(pb[2 * tc + hh][:, 0:T],
                                                        zhis[:, tc * 128:(tc + 1) * 128],
                                                        smb[:, OFF_B2HI + hh * 512:OFF_B2HI + (hh + 1) * 512],
                                                        start=False, stop=True),
                       waits=[ev_zhi] if (tc == 0 and hh == 0) else ())
            ev_ffnmm = S.cur(st)
            ev_fftap = None
            for tc in range(NC_):
                for hh in range(2):
                    w_si = []
                    if tc == 0 and hh == 0:
                        w_si = [ev_ffnmm, fdone["x1c"]]
                    SC(lambda e, tc=tc, hh=hh: e.activation(scrB[:, tc, hh * 512:(hh + 1) * 512],
                                                            pb[2 * tc + hh][:, 0:T], AF.Silu),
                       waits=w_si)
            ev_g = S.cur(ss)
            if debug and g == 0:
                ev_gtap = OD(lambda e: e.dma_start(out=dbg["dbg_g"], in_=scrB[:, :, :]), waits=[ev_g])
            else:
                ev_gtap = None

            ev_qev = []
            for c in range(NC_):
                b = 2 + c % 2
                waits = [fdone["q"], ev_g] if c == 0 else []
                if len(ev_qev) >= 2:
                    waits.append(ev_qev[-2])
                ev_t = TT(lambda e, c=c, b=b: e.transpose(pb[b][0:16, 0:128], qs1[:, c, :], idf[:, :]),
                          waits=waits)
                ev_e = V(lambda e, c=c, b=b: e.tensor_copy(qT[:, c * 128:(c + 1) * 128], pb[b][0:16, 0:128]),
                         waits=[ev_t])
                ev_qev.append(ev_e)
            ev_qT = S.cur(sv)
            ev_h = []
            for jf in range(NJF):
                b = 2 + jf % 2
                waits = [ev_qT, ev_const_g] if jf == 0 else []
                if len(ev_h) >= 2:
                    waits.append(ev_h[-2])
                ev_m = TT(lambda e, jf=jf, b=b: e.matmul(pb[b][:, 0:T],
                                                         w1b[:, jf * 128:(jf + 1) * 128],
                                                         qT[:, :], start=True, stop=True),
                          waits=waits)
                w_r = [ev_m]
                if jf == 0:
                    w_r.append(ev_ffnmm)
                ev_r = SC(lambda e, jf=jf, b=b: e.activation(big[:, jf, :], pb[b][:, 0:T],
                                                             AF.Relu, bias=b1c[:, jf:jf + 1], scale=1.0),
                          waits=w_r)
                ev_h.append(ev_r)
            ev_rT = S.cur(ss)
            for c in range(NC_):
                V(lambda e, c=c: e.tensor_add(x1[:, c, :], x1[:, c, :], b2bc[:, :]),
                  waits=[ev_qT] if c == 0 else ())
            ev_x1b = S.cur(sv)
            ev_mm_by_kf = {}
            ev_w2buf = [None] * 4
            for kf in range(NKF):
                parw = kf % 2
                waits = []
                if kf >= 2:
                    waits.append(ev_mm_by_kf[kf - 2])
                else:
                    waits.append(ev_mm_wd[NKF - 1])
                if g == 0:
                    ev_w2buf[parw] = GD(lambda e, kf=kf, p=parw:
                                        e.dma_start(out=wsb[:, p, :], in_=w2p[kf]), waits=waits)
                else:
                    waits.append(CONV_WDW2_DONE)
                    ev_w2buf[parw] = YD(lambda e, kf=kf, p=parw:
                                        e.dma_start(out=wsb[:, p, :], in_=w2q[kf]), waits=waits)
                for tc in range(NC_):
                    for hh in range(2):
                        w2_ = []
                        if tc == 0 and hh == 0:
                            w2_.append(ev_w2buf[parw])
                            if kf == 0:
                                w2_ += [ev_rT, ev_g]
                        ev_mm = TT(lambda e, kf=kf, tc=tc, hh=hh, p=parw:
                                   e.matmul(pb[2 * tc + hh][:, 0:T],
                                            big[:, kf, tc * 128:(tc + 1) * 128],
                                            wsb[:, p, hh * 512:(hh + 1) * 512],
                                            start=(kf == 0), stop=(kf == NKF - 1)),
                                   waits=w2_)
                ev_mm_by_kf[kf] = ev_mm
            ev_ffmm = S.cur(st)
            ev_wsb_guard = ev_ffmm
            ev_s2 = []
            for c in range(NC_):
                for hh in range(2):
                    w_e = []
                    if c == 0 and hh == 0:
                        w_e = [ev_ffmm, ev_x1b]
                        if ev_fftap is not None:
                            w_e.append(ev_fftap)
                    ev_e = V(lambda e, c=c, hh=hh:
                             e.tensor_tensor(x1[:, c, hh * 512:(hh + 1) * 512],
                                             pb[2 * c + hh][:, 0:T],
                                             x1[:, c, hh * 512:(hh + 1) * 512], ALU.add),
                             waits=w_e)
                    ev_s2.append(ev_e)
            ev_ff = S.cur(sv)
            ev_banks67_prev = ev_ff
            if debug and g == 0:
                OD(lambda e: e.dma_start(out=dbg["dbg_ff"], in_=x1[:, :, :]), waits=[ev_ff])
            base_sv = S.cnt.get(sv, 0)
            n_warm = 18 if g + 1 < G else 14
            for i in range(n_warm):
                TT(lambda e: e.matmul(pb[0][0:8, 0:128], idf[:, 0:8], idf[:, :],
                                      start=True, stop=True, skip_group_check=True),
                   waits=[(sv, base_sv + 3 * (i + 1))], inc=False)
            for c in range(NC_):
                V(lambda e, c=c: e.scalar_tensor_tensor(sfin[:, c, :], x1[:, c, :], 1.0,
                                                        x1[:, c, :], ALU.mult, ALU.mult,
                                                        accum_out=stats[:, 8 + c:9 + c]))
            for c in range(NC_):
                V(lambda e, c=c: e.reduce_sum(stats[:, c:c + 1], x1[:, c, :], axis=AX))
            for c in range(NC_):
                V(lambda e, c=c: e.tensor_scalar(stats[:, 4 + c:5 + c], stats[:, c:c + 1],
                                                 1.0 / D, None, ALU.mult))
            for c in range(NC_):
                V(lambda e, c=c: e.scalar_tensor_tensor(stats[:, 12 + c:13 + c],
                                                        stats[:, 4 + c:5 + c], -float(D),
                                                        stats[:, 4 + c:5 + c], ALU.mult, ALU.mult))
            for c in range(NC_):
                V(lambda e, c=c: e.tensor_add(stats[:, 8 + c:9 + c], stats[:, 8 + c:9 + c],
                                              stats[:, 12 + c:13 + c]))
            ev_v2 = S.cur(sv)
            for c in range(NC_):
                SC(lambda e, c=c: e.activation(stats[:, 12 + c:13 + c], stats[:, 8 + c:9 + c],
                                               AF.Sqrt, bias=cst[:, C_EPS:C_EPS + 1], scale=1.0 / D),
                   waits=[ev_v2] if c == 0 else ())
            ev_sd2 = S.cur(ss)
            for c in range(NC_):
                V(lambda e, c=c: e.reciprocal(stats[:, 12 + c:13 + c], stats[:, 12 + c:13 + c]),
                  waits=[ev_sd2] if c == 0 else ())
            V(lambda e: e.drain(), inc=False)
            for c in range(NC_):
                V(lambda e, c=c: e.tensor_scalar(sfin[:, c, :], x1[:, c, :],
                                                 stats[:, 4 + c:5 + c], stats[:, 12 + c:13 + c],
                                                 ALU.subtract, ALU.mult))
            for c in range(NC_):
                V(lambda e, c=c: e.tensor_mul(sfin[:, c, :], sfin[:, c, :], g2b[:, :]))
            for c in range(NC_):
                V(lambda e, c=c: e.tensor_add(sfin[:, c, :], sfin[:, c, :], be2b[:, :]))
            ev_s = S.cur(sv)
            if debug and g == 0:
                OD(lambda e: e.dma_start(out=dbg["dbg_s"], in_=sfin[:, :, :]), waits=[ev_s])

            xout = scrA
            for c in range(NC_):
                w_cb = [ev_s, ev_g] if c == 0 else []
                if c == 0 and ev_gtap is not None:
                    w_cb.append(ev_gtap)
                V(lambda e, c=c: e.tensor_tensor(xout[:, c, :], scrB[:, c, :], sfin[:, c, :], ALU.mult),
                  waits=w_cb)
            for c in range(NC_):
                V(lambda e, c=c: e.tensor_tensor(xout[:, c, :], xout[:, c, :], xr[:, c, :], ALU.add))
            ev_outv = S.cur(sv)
            orow = od[g * T:(g + 1) * T, :].rearrange("(p c) d -> p c d", c=NC_)
            ev_odma = OD(lambda e, r=orow: e.dma_start(out=r, in_=xout[:, :, :]), waits=[ev_outv])
            ev_odma_prev = ev_odma
            ev_scrb_guard = ev_gtap
            ev_outv_prev = ev_outv

        S.items["sync"].append((lambda e: None, [S.cur(so), S.cur(sg), S.cur(sqw), S.cur(sx)], None))

        with nc.Block() as block:
            @block.sync
            def _(eng):
                for fn, waits, ev in S.items["sync"]:
                    for (sem, val) in waits:
                        eng.wait_ge(sem, val)
                    inst = fn(eng)
                    if ev is not None:
                        inst.then_inc(ev[0], 1)

            @block.gpsimd
            def _(eng):
                S.emit("gpsimd", eng)

            @block.tensor
            def _(eng):
                S.emit("tensor", eng)

            @block.vector
            def _(eng):
                S.emit("vector", eng)

            @block.scalar
            def _(eng):
                S.emit("scalar", eng)

    return nc


def _prep_host(inputs):
    w_gate = np.asarray(inputs["L0_ffn_w_gate"], np.float32)
    w_up = np.asarray(inputs["L0_ffn_w_up"], np.float32)
    w_down = np.asarray(inputs["L0_ffn_w_down"], np.float32)
    w2 = np.asarray(inputs["L0_attn_w2"], np.float32)
    w1 = np.asarray(inputs["L0_attn_w1"], np.float32)
    gate_w = np.asarray(inputs["L0_ffn_gate_w"], np.float32)
    A1 = np.asarray(inputs["L0_ffn_A1"], np.float32)
    B1 = np.asarray(inputs["L0_ffn_B1"], np.float32)
    A3 = np.asarray(inputs["L0_ffn_A3"], np.float32)
    B3 = np.asarray(inputs["L0_ffn_B3"], np.float32)
    A2 = np.asarray(inputs["L0_ffn_A2"], np.float32)
    B2 = np.asarray(inputs["L0_ffn_B2"], np.float32)

    wjp = np.zeros((NJF, 128, WJ_COLS), np.float32)
    wg4 = w_gate.reshape(NJF, 128, NKD, 128).transpose(0, 3, 2, 1)
    wjp[:, :, OFF_WG:OFF_WG + 1024] = wg4.reshape(NJF, 128, NKD * 128)
    wu4 = w_up.reshape(NJF, 128, NKD, 128).transpose(0, 3, 2, 1)
    wjp[:, :, OFF_WU:OFF_WU + 1024] = wu4.reshape(NJF, 128, NKD * 128)
    b1r = B1.reshape(E, NJF, 128, R).transpose(0, 1, 3, 2)
    b3r = B3.reshape(E, NJF, 128, R).transpose(0, 1, 3, 2)
    for ee in range(E):
        rr0 = 64 * (ee // 4) + 16 * (ee % 4)
        wjp[:, rr0:rr0 + 16, OFF_B1 + 128 * ee:OFF_B1 + 128 * (ee + 1)] = b1r[ee]
        wjp[:, rr0:rr0 + 16, OFF_B3 + 128 * ee:OFF_B3 + 128 * (ee + 1)] = b3r[ee]
    a2r = A2.reshape(E, R, NJF, 128).transpose(2, 3, 0, 1)
    a2p = np.zeros((NJF, 128, E, 32), np.float32)
    a2p[:, :, :, :R] = a2r
    wjp[:, :, OFF_A2:OFF_A2 + 256] = a2p.reshape(NJF, 128, 256)

    w2p = np.ascontiguousarray(w2.T.reshape(NKF, 128, 1024))
    wdp = np.ascontiguousarray(w_down.T.reshape(NKF, 128, 1024))

    spk = np.zeros((128, SP_COLS), np.float32)

    def astack(Ax):
        tight = Ax.reshape(E * R, D)
        return tight.T.reshape(NKD, 128, 128).transpose(1, 0, 2).reshape(128, NKD * 128)

    spk[:, OFF_A1:OFF_A1 + 1024] = astack(A1)
    spk[:, OFF_A3:OFF_A3 + 1024] = astack(A3)
    for eo in range(4):
        spk[32 * eo:32 * eo + 16, OFF_B2LO:OFF_B2LO + 1024] = B2[eo].T
        spk[32 * eo:32 * eo + 16, OFF_B2HI:OFF_B2HI + 1024] = B2[4 + eo].T
    gt = gate_w.T.reshape(NKD, 128, 8).transpose(1, 0, 2).reshape(128, NKD * 8)
    spk[:, OFF_GATE:OFF_GATE + 64] = gt

    w1tp = w1.reshape(NJF, 128, R).transpose(2, 0, 1).reshape(16, FF).copy()

    b1c = np.asarray(inputs["L0_attn_b1"], np.float32).reshape(32, 128).T.copy()
    b2c = np.ascontiguousarray(np.broadcast_to(np.asarray(inputs["L0_attn_b2"], np.float32), (128, D)))
    g1b = np.ascontiguousarray(np.broadcast_to(np.asarray(inputs["L0_attn_gamma1"], np.float32), (128, D)))
    be1b = np.ascontiguousarray(np.broadcast_to(np.asarray(inputs["L0_attn_beta1"], np.float32), (128, D)))
    g2b = np.ascontiguousarray(np.broadcast_to(np.asarray(inputs["L0_attn_gamma2"], np.float32), (128, D)))
    be2b = np.ascontiguousarray(np.broadcast_to(np.asarray(inputs["L0_attn_beta2"], np.float32), (128, D)))
    idf = np.eye(128, dtype=np.float32)
    sel = np.zeros((8, 8, 128), np.float32)
    for ee in range(8):
        sel[ee, ee, :] = 1.0
    sel = sel.reshape(8, 1024)
    cst = np.zeros((128, 8), np.float32)
    cst[:, C_EPS] = EPS
    cst[:, C_PIH] = SIN_BIAS
    cst[:, C_QTR] = 0.25
    cst[:, C_MAG] = MAGIC
    cst[:, C_NMAG] = -MAGIC

    return dict(wjp=wjp, w2p=w2p, wdp=wdp, spk=spk, w1t=w1tp, b1d=b1c, b2d=b2c,
                g1d=g1b, be1d=be1b, g2d=g2b, be2d=be2b, idfd=idf,
                seld=sel, cstd=cst, gtd=np.ascontiguousarray(gt))


def kernel(**inputs):
    assert inputs["L0_attn_attn_circuit_params"] is not None
    assert inputs["L0_attn_ffn_circuit_params"] is not None
    x = np.ascontiguousarray(np.asarray(inputs["L0_attn_x"], np.float32).reshape(16384, D))
    shared = _prep_host(inputs)
    nc = build(debug=False)
    in_maps = []
    for i in range(8):
        m = dict(shared)
        xc = x[i * PC:(i + 1) * PC]
        m["x"] = np.ascontiguousarray(
            xc.reshape(G, NC_, 128, D).transpose(0, 2, 1, 3).reshape(PC, D))
        in_maps.append(m)
    res = run_bass_kernel_spmd(nc, in_maps, core_ids=list(range(8)))
    outs = []
    for i in range(8):
        oc = np.asarray(res.results[i]["o"])
        outs.append(oc.reshape(G, 128, NC_, D).transpose(0, 2, 1, 3).reshape(PC, D))
    out = np.concatenate(outs, axis=0)
    return out.reshape(4, 4096, D)


# version 26
# speedup vs baseline: 1.0120x; 1.0106x over previous
import contextlib
import numpy as np
import concourse.bass as bass
import concourse.mybir as mybir
from concourse.bass_utils import run_bass_kernel_spmd

F32 = mybir.dt.float32
BF16 = mybir.dt.bfloat16
AF = mybir.ActivationFunctionType
ALU = mybir.AluOpType
AX = mybir.AxisListType.X

PI = float(np.pi)
C2PI = float(np.float32(2.0 * np.pi))
MAGIC = 3.0 * 4194304.0
SIN_BIAS = float(np.float32(np.pi / 2))
RR_HI = float(np.float32(np.pi - 5e-5 - np.pi / 2))
RR_LO = float(np.float32(-np.pi + 5e-5 - np.pi / 2))
EPS = 1e-5

D = 1024
FF = 4096
E = 8
R = 16
NQ = 16
PC = 2048
T = 512
G = PC // T
NC_ = 4
NKD = D // 128
NJF = FF // 128
NKF = FF // 128

OFF_WG = 0
OFF_WU = 1024
OFF_B1 = 2048
OFF_B3 = 3072
OFF_A2 = 4096
WJ_COLS = 4096 + 256

OFF_A1 = 0
OFF_A3 = 1024
OFF_B2LO = 2048
OFF_B2HI = 3072
OFF_GATE = 4096
SP_COLS = 4096 + 64

C_EPS = 0
C_PIH = 1
C_QTR = 2
C_MAG = 3
C_NMAG = 4


class Sched:
    def __init__(self):
        self.items = {k: [] for k in ("sync", "gpsimd", "tensor", "vector", "scalar")}
        self.cnt = {}

    def ev(self, sem, amt=1):
        self.cnt[sem] = self.cnt.get(sem, 0) + amt
        return (sem, self.cnt[sem])

    def cur(self, sem):
        return (sem, self.cnt.get(sem, 0))

    def op(self, engine, fn, waits=(), inc=None):
        ev = self.ev(inc, 1) if inc is not None else None
        self.items[engine].append((fn, list(waits), ev))
        return ev

    def dma(self, engine, fn, sem, waits=()):
        ev = self.ev(sem, 16)
        self.items[engine].append(
            (lambda e, f=fn, s=sem: f(e).then_inc(s, 16), list(waits), None))
        return ev

    def emit(self, engine, eng):
        for fn, waits, ev in self.items[engine]:
            for (sem, val) in waits:
                eng.wait_ge(sem, val)
            inst = fn(eng)
            if ev is not None:
                inst.then_inc(ev[0], 1)


def build(debug=False):
    nc = bass.Bass("TRN2", target_bir_lowering=False, detect_race_conditions=False)

    xd = nc.dram_tensor("x", [PC, D], F32, kind="ExternalInput").ap()
    wjp = nc.dram_tensor("wjp", [NJF, 128, WJ_COLS], F32, kind="ExternalInput").ap()
    w2p = nc.dram_tensor("w2p", [NKF, 128, 1024], F32, kind="ExternalInput").ap()
    wdp = nc.dram_tensor("wdp", [NKF, 128, 1024], F32, kind="ExternalInput").ap()
    spk = nc.dram_tensor("spk", [128, SP_COLS], F32, kind="ExternalInput").ap()
    w1t = nc.dram_tensor("w1t", [16, FF], F32, kind="ExternalInput").ap()
    b1d = nc.dram_tensor("b1d", [128, 32], F32, kind="ExternalInput").ap()
    b2d = nc.dram_tensor("b2d", [128, D], F32, kind="ExternalInput").ap()
    g1d = nc.dram_tensor("g1d", [128, D], F32, kind="ExternalInput").ap()
    be1d = nc.dram_tensor("be1d", [128, D], F32, kind="ExternalInput").ap()
    g2d = nc.dram_tensor("g2d", [128, D], F32, kind="ExternalInput").ap()
    be2d = nc.dram_tensor("be2d", [128, D], F32, kind="ExternalInput").ap()
    idfd = nc.dram_tensor("idfd", [128, 128], F32, kind="ExternalInput").ap()
    seld = nc.dram_tensor("seld", [8, 1024], F32, kind="ExternalInput").ap()
    gtd = nc.dram_tensor("gtd", [128, 64], F32, kind="ExternalInput").ap()
    cstd = nc.dram_tensor("cstd", [128, 8], F32, kind="ExternalInput").ap()
    od = nc.dram_tensor("o", [PC, D], F32, kind="ExternalOutput").ap()
    wdq = nc.dram_tensor("wdq", [NKF, 128, 1024], BF16, kind="Internal").ap()
    w2q = nc.dram_tensor("w2q", [NKF, 128, 1024], BF16, kind="Internal").ap()

    dbg = {}
    if debug:
        for nm, shp, dt in [
            ("dbg_xn", [128, NC_, D], F32), ("dbg_x1", [128, NC_, D], F32),
            ("dbg_s", [128, NC_, D], F32), ("dbg_probs", [128, NC_, 8], F32),
            ("dbg_wgt", [128, NC_, 8], F32), ("dbg_cw1", [128, T], F32),
            ("dbg_cw3", [128, T], F32), ("dbg_u1", [128, T], F32),
            ("dbg_sact", [128, T], F32), ("dbg_mix", [128, NJF, T], F32),
            ("dbg_zlo", [128, T], F32), ("dbg_zhi", [128, T], F32),
            ("dbg_ff", [128, NKD, T], F32), ("dbg_g", [128, NKD, T], F32),
            ("dbg_ao", [128, NC_, D], F32), ("dbg_q", [128, NC_, NQ], F32),
            ("dbg_t1lo", [128, T], F32), ("dbg_rt", [128, NJF, T], F32),
        ]:
            dbg[nm] = nc.dram_tensor(nm, shp, dt, kind="ExternalOutput").ap()

    S = Sched()
    es = contextlib.ExitStack()
    with es:
        def sb(name, shape, dt):
            return es.enter_context(nc.sbuf_tensor(name, shape, dt))

        cst = sb("cst", [128, 8], F32)
        b1c = sb("b1c", [128, 32], F32)
        b2bc = sb("b2bc", [128, D], F32)
        g1b = sb("g1b", [128, D], F32)
        be1b = sb("be1b", [128, D], F32)
        g2b = sb("g2b", [128, D], F32)
        be2b = sb("be2b", [128, D], F32)
        idf = sb("idf", [128, 128], F32)
        idb = sb("idb", [128, 128], BF16)
        selb = sb("selb", [8, 1024], BF16)
        smb = sb("smb", [128, SP_COLS], BF16)
        w1b = sb("w1b", [16, FF], BF16)

        xr = sb("xr", [128, NC_, D], F32)
        xn = sb("xn", [128, NC_, D], F32)
        sfin = sb("sfin", [128, NC_, D], F32)
        scrA = sb("scrA", [128, NC_, D], F32)
        scrB = sb("scrB", [128, NC_, D], F32)
        xnT = sb("xnT", [128, NKD, T], BF16)
        big = sb("big", [128, NJF, T], BF16)
        stats = sb("stats", [128, 16], F32)
        gprob = sb("gprob", [128, NC_, 8], F32)
        gwc = sb("gwc", [128, NC_, 8], F32)
        mx8c = sb("mx8c", [128, NC_, 8], F32)
        wgt = sb("wgt", [128, NC_, 8], F32)
        wgtT = sb("wgtT", [8, T], BF16)
        wbc = sb("wbc", [128, E, T], BF16)
        qT = sb("qT", [16, T], BF16)
        gtf = sb("gtf", [128, 64], F32)
        xnf = sb("xnf", [128, 3, 128], F32)
        qs1 = sb("qs1", [128, NC_, NQ], F32)
        qs2 = sb("qs2", [128, NC_, NQ], F32)
        cw1s = sb("cw1s", [128, 2, T], BF16)
        cw3s = sb("cw3s", [128, 2, T], BF16)
        u1b = sb("u1b", [128, 2, T], BF16)
        u3b = sb("u3b", [128, 2, T], BF16)
        sab = sb("sab", [128, 3, T], BF16)
        t1s = sb("t1s", [128, T], BF16)
        t3s = sb("t3s", [128, T], BF16)
        zlos = sb("zlos", [128, T], BF16)
        zhis = sb("zhis", [128, T], BF16)
        wjb = sb("wjb", [128, 2, WJ_COLS], BF16)
        wsb = sb("wsb", [128, 2, 1024], BF16)

        pb = [es.enter_context(nc.psum_tensor(f"pb{i}", [128, 512], F32)) for i in range(8)]

        sy = es.enter_context(nc.semaphore("sy"))
        sg = es.enter_context(nc.semaphore("sg"))
        sgc = es.enter_context(nc.semaphore("sgc"))
        st = es.enter_context(nc.semaphore("st"))
        sv = es.enter_context(nc.semaphore("sv"))
        ss = es.enter_context(nc.semaphore("ss"))
        so = es.enter_context(nc.semaphore("so"))
        sqw = es.enter_context(nc.semaphore("sqw"))
        sx = es.enter_context(nc.semaphore("sx"))

        def YD(fn, waits=()):
            return S.dma("sync", fn, sy, waits)

        def OD(fn, waits=()):
            return S.dma("sync", fn, so, waits)

        def GD(fn, waits=()):
            return S.dma("gpsimd", fn, sg, waits)

        def QW(fn, waits=()):
            return S.dma("gpsimd", fn, sqw, waits)

        def XD(fn, waits=()):
            return S.dma("gpsimd", fn, sx, waits)

        def GC(fn, waits=(), inc=True):
            return S.op("gpsimd", fn, waits, sgc if inc else None)

        def TT(fn, waits=(), inc=True):
            return S.op("tensor", fn, waits, st if inc else None)

        def V(fn, waits=(), inc=True):
            return S.op("vector", fn, waits, sv if inc else None)

        def SC(fn, waits=(), inc=True):
            return S.op("scalar", fn, waits, ss if inc else None)

        YD(lambda e: e.dma_start(out=cst[:, :], in_=cstd))
        YD(lambda e: e.dma_start(out=b1c[:, :], in_=b1d))
        YD(lambda e: e.dma_start(out=b2bc[:, :], in_=b2d))
        YD(lambda e: e.dma_start(out=g1b[:, :], in_=g1d))
        YD(lambda e: e.dma_start(out=be1b[:, :], in_=be1d))
        YD(lambda e: e.dma_start(out=g2b[:, :], in_=g2d))
        YD(lambda e: e.dma_start(out=be2b[:, :], in_=be2d))
        YD(lambda e: e.dma_start(out=idf[:, :], in_=idfd))
        YD(lambda e: e.dma_start(out=gtf[:, :], in_=gtd))
        GD(lambda e: e.dma_start(out=idb[:, :], in_=idfd))
        GD(lambda e: e.dma_start(out=selb[:, :], in_=seld))
        GD(lambda e: e.dma_start(out=smb[:, :], in_=spk))
        GD(lambda e: e.dma_start(out=w1b[:, :], in_=w1t))
        ev_const_y = S.cur(sy)
        ev_const_g = S.cur(sg)

        def _conv_gen():
            for ch in range(8):
                QW(lambda e, ch=ch: e.dma_start(
                    out=wdq[4 * ch:4 * ch + 4], in_=wdp[4 * ch:4 * ch + 4]))
                yield
            for ch in range(8):
                QW(lambda e, ch=ch: e.dma_start(
                    out=w2q[4 * ch:4 * ch + 4], in_=w2p[4 * ch:4 * ch + 4]))
                yield

        conv_state = {"gen": _conv_gen(), "live": True}
        CONV_WDW2_DONE = (sqw, 16 * 16)

        ev_zlo_prev = None
        ev_zhi_prev = None
        ev_odma_prev = None
        ev_outv_prev = None
        ev_scrb_guard = None
        ev_banks67_prev = None
        ev_wsb_guard = None
        ev_wj_used = [None, None]

        for g in range(G):
            x1 = xn
            rows = xd[g * T:(g + 1) * T, :].rearrange("(p c) d -> p c d", c=NC_)
            w_x = [] if ev_outv_prev is None else [ev_outv_prev]
            ev_x = XD(lambda e, r=rows: e.dma_start(out=xr[:, :, :], in_=r), waits=w_x)
            wait_in = [ev_x, ev_const_y, ev_const_g]

            for c in range(NC_):
                V(lambda e, c=c: e.scalar_tensor_tensor(xn[:, c, :], xr[:, c, :], 1.0, xr[:, c, :],
                                                        ALU.mult, ALU.mult,
                                                        accum_out=stats[:, 8 + c:9 + c]),
                  waits=wait_in if c == 0 else ())
            for c in range(NC_):
                V(lambda e, c=c: e.reduce_sum(stats[:, c:c + 1], xr[:, c, :], axis=AX))
            for c in range(NC_):
                V(lambda e, c=c: e.tensor_scalar(stats[:, 4 + c:5 + c], stats[:, c:c + 1],
                                                 1.0 / D, None, ALU.mult))
            for c in range(NC_):
                V(lambda e, c=c: e.scalar_tensor_tensor(stats[:, 12 + c:13 + c],
                                                        stats[:, 4 + c:5 + c], -float(D),
                                                        stats[:, 4 + c:5 + c], ALU.mult, ALU.mult))
            for c in range(NC_):
                V(lambda e, c=c: e.tensor_add(stats[:, 8 + c:9 + c], stats[:, 8 + c:9 + c],
                                              stats[:, 12 + c:13 + c]))
            ev_var = S.cur(sv)
            for c in range(NC_):
                SC(lambda e, c=c: e.activation(stats[:, 12 + c:13 + c], stats[:, 8 + c:9 + c],
                                               AF.Sqrt, bias=cst[:, C_EPS:C_EPS + 1], scale=1.0 / D),
                   waits=[ev_var] if c == 0 else ())
            ev_sd = S.cur(ss)
            for c in range(NC_):
                V(lambda e, c=c: e.reciprocal(stats[:, 12 + c:13 + c], stats[:, 12 + c:13 + c]),
                  waits=[ev_sd] if c == 0 else ())
            V(lambda e: e.drain(), inc=False)
            ev_xn_c = []
            for c in range(NC_):
                ev_xn_c.append(V(lambda e, c=c: e.tensor_scalar(
                    xn[:, c, :], xr[:, c, :],
                    stats[:, 4 + c:5 + c], stats[:, 12 + c:13 + c],
                    ALU.subtract, ALU.mult)))
            ev_xn = S.cur(sv)
            if debug and g == 0:
                OD(lambda e: e.dma_start(out=dbg["dbg_xn"], in_=xn[:, :, :]), waits=[ev_xn])

            ev_tr = []
            ev_gcopy = []
            ev_gm_hist = []
            ev_f_hist = []

            def emit_gate_mm(i2):
                c2, kd2 = divmod(i2, NKD)
                gb2 = 4 + c2 % 2
                w_g = [ev_f_hist[i2]]
                if kd2 == 0 and c2 >= 2:
                    w_g.append(ev_gcopy[c2 - 2])
                ev_gm = TT(lambda e, kd=kd2, gb=gb2, par=i2 % 3:
                           e.matmul(pb[gb][0:128, 0:8], xnf[:, par, :],
                                    gtf[:, kd * 8:(kd + 1) * 8],
                                    start=(kd == 0), stop=(kd == NKD - 1)),
                           waits=w_g)
                ev_gm_hist.append(ev_gm)
                if kd2 == NKD - 1:
                    ev_cp = V(lambda e, c=c2, gb=gb2:
                              e.tensor_copy(gprob[:, c, :], pb[gb][0:128, 0:8]),
                              waits=[S.cur(st)])
                    ev_gcopy.append(ev_cp)

            for c in range(NC_):
                for kd in range(NKD):
                    i = c * NKD + kd
                    b = (2, 3, 6, 7)[i % 4]
                    par = i % 3
                    waits = [ev_xn_c[c]] if kd == 0 else []
                    if i in (2, 3) and ev_banks67_prev is not None:
                        waits.append(ev_banks67_prev)
                    if len(ev_tr) >= 4:
                        waits.append(ev_tr[-4][1])
                    ev_t = TT(lambda e, c=c, kd=kd, b=b:
                              e.transpose(pb[b][0:128, 0:128],
                                          xn[:, c, kd * 128:(kd + 1) * 128], idf[:, :]),
                              waits=waits)
                    ev_e = V(lambda e, c=c, kd=kd, b=b:
                             e.tensor_copy(xnT[:, kd, c * 128:(c + 1) * 128], pb[b][0:128, 0:128]),
                             waits=[ev_t])
                    w_f = []
                    if i >= 3:
                        w_f.append(ev_gm_hist[i - 3])
                    ev_f = V(lambda e, b=b, par=par:
                             e.tensor_copy(xnf[:, par, :], pb[b][0:128, 0:128]),
                             waits=w_f)
                    ev_f_hist.append(ev_f)
                    ev_tr.append((ev_e, ev_f))
                    if i >= 2:
                        emit_gate_mm(i - 2)
            emit_gate_mm(30)
            emit_gate_mm(31)
            ev_xnT = S.cur(sv)

            ev_glog = ev_gcopy[-1]
            for c in range(NC_):
                V(lambda e, c=c: e.reduce_max(stats[:, c:c + 1], gprob[:, c, :], axis=AX),
                  waits=[ev_glog] if c == 0 else ())
            V(lambda e: e.drain(), inc=False)
            for c in range(NC_):
                V(lambda e, c=c: e.tensor_scalar(stats[:, 4 + c:5 + c], stats[:, c:c + 1],
                                                 -1.0, None, ALU.mult))
            ev_gmax = S.cur(sv)
            for c in range(NC_):
                SC(lambda e, c=c: e.activation(gprob[:, c, :], gprob[:, c, :], AF.Exp,
                                               bias=stats[:, 4 + c:5 + c], scale=1.0),
                   waits=[ev_gmax] if c == 0 else ())
            ev_gexp = S.cur(ss)
            for c in range(NC_):
                V(lambda e, c=c: e.reduce_sum(stats[:, 8 + c:9 + c], gprob[:, c, :], axis=AX),
                  waits=[ev_gexp] if c == 0 else ())
            V(lambda e: e.drain(), inc=False)
            for c in range(NC_):
                V(lambda e, c=c: e.reciprocal(stats[:, 12 + c:13 + c], stats[:, 8 + c:9 + c]))
            V(lambda e: e.drain(), inc=False)
            for c in range(NC_):
                V(lambda e, c=c: e.tensor_scalar(gprob[:, c, :], gprob[:, c, :],
                                                 stats[:, 12 + c:13 + c], None, ALU.mult))
            V(lambda e: e.drain(), inc=False)
            ev_ptap = None
            if debug and g == 0:
                ev_p_ = S.cur(sv)
                ev_ptap = OD(lambda e: e.dma_start(out=dbg["dbg_probs"], in_=gprob[:, :, :]), waits=[ev_p_])
            for c in range(NC_):
                V(lambda e, c=c: e.max(out=mx8c[:, c, :], in_=gprob[:, c, :]),
                  waits=[ev_ptap] if (c == 0 and ev_ptap is not None) else ())
            V(lambda e: e.drain(), inc=False)
            for c in range(NC_):
                V(lambda e, c=c: e.tensor_add(stats[:, c:c + 1], mx8c[:, c, 0:1], mx8c[:, c, 1:2]))
            for c in range(NC_):
                V(lambda e, c=c: e.tensor_scalar(gwc[:, c, :], gprob[:, c, :], mx8c[:, c, 1:2],
                                                 -1e30, ALU.subtract, ALU.mult))
            V(lambda e: e.drain(), inc=False)
            for c in range(NC_):
                V(lambda e, c=c: e.reciprocal(stats[:, 4 + c:5 + c], stats[:, c:c + 1]))
            for c in range(NC_):
                V(lambda e, c=c: e.tensor_scalar(gwc[:, c, :], gwc[:, c, :], 1.0, 0.0, ALU.min, ALU.max))
            V(lambda e: e.drain(), inc=False)
            for c in range(NC_):
                V(lambda e, c=c: e.tensor_scalar(gwc[:, c, :], gwc[:, c, :], -1.0, 1.0, ALU.mult, ALU.add))
            for c in range(NC_):
                V(lambda e, c=c: e.tensor_scalar(gprob[:, c, :], gprob[:, c, :],
                                                 stats[:, 4 + c:5 + c], None, ALU.mult))
            V(lambda e: e.drain(), inc=False)
            for c in range(NC_):
                V(lambda e, c=c: e.tensor_mul(wgt[:, c, :], gprob[:, c, :], gwc[:, c, :]))
            V(lambda e: e.drain(), inc=False)
            ev_wgt = S.cur(sv)
            if debug and g == 0:
                OD(lambda e: e.dma_start(out=dbg["dbg_wgt"], in_=wgt[:, :, :]), waits=[ev_wgt])

            wbc_state = {}

            def emit_wgt_t():
                ev_wt = []
                for c in range(NC_):
                    b = 6 + c % 2
                    waits = [ev_wgt, ev_tst] if c == 0 else []
                    if len(ev_wt) >= 2:
                        waits.append(ev_wt[-2])
                    ev_t = TT(lambda e, c=c, b=b: e.transpose(pb[b][0:8, 0:128],
                                                              wgt[:, c, :], idf[:, :]),
                              waits=waits)
                    ev_e = V(lambda e, c=c, b=b: e.tensor_copy(wgtT[:, c * 128:(c + 1) * 128],
                                                               pb[b][0:8, 0:128]),
                             waits=[ev_t])
                    ev_wt.append(ev_e)
                wbc_state["wgtT"] = S.cur(sv)

            def emit_wbc():
                ev_wb = []
                for ee in range(E):
                    b = 6 + ee % 2
                    waits = [wbc_state["wgtT"]] if ee == 0 else []
                    if len(ev_wb) >= 2:
                        waits.append(ev_wb[-2])
                    ev_m = TT(lambda e, ee=ee, b=b: e.matmul(pb[b][:, 0:T],
                                                             selb[:, ee * 128:(ee + 1) * 128],
                                                             wgtT[:, :], start=True, stop=True),
                              waits=waits)
                    ev_e = V(lambda e, ee=ee, b=b: e.tensor_copy(wbc[:, ee, :], pb[b][:, 0:T]),
                             waits=[ev_m])
                    ev_wb.append(ev_e)
                wbc_state["wbc"] = S.cur(sv)

            ev_tst_l = []
            for i, (buf, off) in enumerate([(t1s, OFF_A1), (t3s, OFF_A3)]):
                b = 6 + i % 2
                for kd in range(NKD):
                    w = []
                    if i == 0 and kd == 0:
                        w = [ev_xnT, ev_const_g]
                        if ev_banks67_prev is not None:
                            w.append(ev_banks67_prev)
                    TT(lambda e, kd=kd, off=off, b=b:
                       e.matmul(pb[b][:, 0:T], smb[:, off + kd * 128:off + (kd + 1) * 128],
                                xnT[:, kd, :], start=(kd == 0), stop=(kd == NKD - 1)),
                       waits=w)
                ev_m = S.cur(st)
                ev_e = V(lambda e, buf=buf, b=b: e.tensor_copy(buf[:, :], pb[b][:, 0:T]),
                         waits=[ev_m])
                ev_tst_l.append(ev_e)
            ev_tst = S.cur(sv)
            if debug and g == 0:
                GD(lambda e: e.dma_start(out=dbg["dbg_t1lo"], in_=t1s[:, :]), waits=[ev_tst])

            fdone = {}

            def attn_gen():
                w0 = [ev_xn]
                if ev_outv_prev is not None:
                    w0.append(ev_outv_prev)
                if ev_scrb_guard is not None:
                    w0.append(ev_scrb_guard)
                for c in range(NC_):
                    SC(lambda e, c=c: e.activation(scrA[:, c, :], xn[:, c, :], AF.Identity,
                                                   bias=cst[:, C_QTR:C_QTR + 1], scale=1.0 / C2PI),
                       waits=w0 if c == 0 else ())
                    yield
                for c in range(NC_):
                    SC(lambda e, c=c: e.activation(scrB[:, c, :], scrA[:, c, :], AF.Identity,
                                                   bias=cst[:, C_MAG:C_MAG + 1], scale=1.0))
                    yield
                for c in range(NC_):
                    SC(lambda e, c=c: e.activation(scrB[:, c, :], scrB[:, c, :], AF.Identity,
                                                   bias=cst[:, C_NMAG:C_NMAG + 1], scale=1.0))
                    yield
                ev_st2 = S.cur(ss)
                for c in range(NC_):
                    V(lambda e, c=c: e.scalar_tensor_tensor(scrA[:, c, :], scrB[:, c, :], -C2PI,
                                                            xn[:, c, :], ALU.mult, ALU.add),
                      waits=[ev_st2] if c == 0 else ())
                    yield
                for c in range(NC_):
                    V(lambda e, c=c: e.tensor_scalar(scrA[:, c, :], scrA[:, c, :],
                                                     RR_HI, RR_LO, ALU.min, ALU.max))
                    yield
                ev_rr = S.cur(sv)
                for c in range(NC_):
                    SC(lambda e, c=c: e.activation(scrB[:, c, :], scrA[:, c, :], AF.Sin,
                                                   bias=cst[:, C_PIH:C_PIH + 1], scale=1.0),
                       waits=[ev_rr] if c == 0 else ())
                    yield
                ev_cos = S.cur(ss)
                for c in range(NC_):
                    for h in range(0, 16, 4):
                        for hh2 in range(h, h + 4):
                            V(lambda e, c=c, h=hh2: e.tensor_tensor_scan(
                                scrA[:, c, h * 64:(h + 1) * 64], scrB[:, c, h * 64:(h + 1) * 64],
                                scrB[:, c, h * 64:(h + 1) * 64], 1.0, ALU.mult, ALU.bypass),
                              waits=[ev_cos] if (c == 0 and hh2 == 0) else ())
                        yield
                ev_ao = S.cur(sv)
                if debug and g == 0:
                    OD(lambda e: e.dma_start(out=dbg["dbg_ao"], in_=scrA[:, :, :]), waits=[ev_ao])
                for c in range(NC_):
                    V(lambda e, c=c: e.tensor_add(scrB[:, c, :], xn[:, c, :], scrA[:, c, :]))
                    yield
                for c in range(NC_):
                    V(lambda e, c=c: e.reduce_sum(stats[:, c:c + 1], scrB[:, c, :], axis=AX))
                    yield
                for c in range(NC_):
                    V(lambda e, c=c: e.tensor_scalar(stats[:, 4 + c:5 + c], stats[:, c:c + 1],
                                                     1.0 / D, None, ALU.mult))
                for c in range(NC_):
                    V(lambda e, c=c: e.tensor_scalar(x1[:, c, :], scrB[:, c, :],
                                                     stats[:, 4 + c:5 + c], None, ALU.subtract),
                      waits=[ev_xnT] if c == 0 else ())
                    yield
                ev_c1 = S.cur(sv)
                fdone["x1c"] = ev_c1
                for c in range(NC_):
                    SC(lambda e, c=c: e.activation(scrA[:, c, :], x1[:, c, :], AF.Square),
                       waits=[ev_c1] if c == 0 else ())
                    yield
                ev_sq1 = S.cur(ss)
                for c in range(NC_):
                    V(lambda e, c=c: e.reduce_sum(stats[:, 8 + c:9 + c], scrA[:, c, :], axis=AX),
                      waits=[ev_sq1] if c == 0 else ())
                    yield
                ev_v1 = S.cur(sv)
                for c in range(NC_):
                    SC(lambda e, c=c: e.activation(stats[:, 12 + c:13 + c], stats[:, 8 + c:9 + c],
                                                   AF.Sqrt, bias=cst[:, C_EPS:C_EPS + 1], scale=1.0 / D),
                       waits=[ev_v1] if c == 0 else ())
                    yield
                ev_sd1 = S.cur(ss)
                for c in range(NC_):
                    V(lambda e, c=c: e.reciprocal(stats[:, 12 + c:13 + c], stats[:, 12 + c:13 + c]),
                      waits=[ev_sd1] if c == 0 else ())
                V(lambda e: e.drain(), inc=False)
                yield
                for c in range(NC_):
                    V(lambda e, c=c: e.tensor_scalar(x1[:, c, :], x1[:, c, :],
                                                     stats[:, 12 + c:13 + c], None, ALU.mult))
                    yield
                for c in range(NC_):
                    V(lambda e, c=c: e.tensor_mul(x1[:, c, :], x1[:, c, :], g1b[:, :]))
                    yield
                for c in range(NC_):
                    V(lambda e, c=c: e.tensor_add(x1[:, c, :], x1[:, c, :], be1b[:, :]))
                    yield
                ev_x1 = S.cur(sv)
                fdone["x1"] = ev_x1
                if debug and g == 0:
                    OD(lambda e: e.dma_start(out=dbg["dbg_x1"], in_=x1[:, :, :]), waits=[ev_x1])
                for c in range(NC_):
                    V(lambda e, c=c: e.tensor_scalar(qs1[:, c, :], x1[:, c, 0:NQ],
                                                     1.0 / C2PI, 0.25, ALU.mult, ALU.add),
                      waits=[ev_x1] if c == 0 else ())
                for c in range(NC_):
                    V(lambda e, c=c: e.tensor_scalar(qs2[:, c, :], qs1[:, c, :],
                                                     MAGIC, -MAGIC, ALU.add, ALU.add))
                yield
                for c in range(NC_):
                    V(lambda e, c=c: e.scalar_tensor_tensor(qs1[:, c, :], qs2[:, c, :], -C2PI,
                                                            x1[:, c, 0:NQ], ALU.mult, ALU.add))
                for c in range(NC_):
                    V(lambda e, c=c: e.tensor_scalar(qs1[:, c, :], qs1[:, c, :],
                                                     RR_HI, RR_LO, ALU.min, ALU.max))
                ev_qrr = S.cur(sv)
                yield
                for c in range(NC_):
                    SC(lambda e, c=c: e.activation(qs2[:, c, :], qs1[:, c, :], AF.Sin,
                                                   bias=cst[:, C_PIH:C_PIH + 1], scale=1.0),
                       waits=[ev_qrr] if c == 0 else ())
                ev_qcos = S.cur(ss)
                yield
                for c in range(NC_):
                    V(lambda e, c=c: e.tensor_tensor_scan(qs1[:, c, :], qs2[:, c, :], qs2[:, c, :],
                                                          1.0, ALU.mult, ALU.bypass),
                      waits=[ev_qcos] if c == 0 else ())
                ev_q = S.cur(sv)
                fdone["q"] = ev_q
                if debug and g == 0:
                    OD(lambda e: e.dma_start(out=dbg["dbg_q"], in_=qs1[:, :, :]), waits=[ev_q])

            gen = attn_gen()
            gen_live = True

            def pop_filler():
                nonlocal gen_live
                if gen_live:
                    try:
                        next(gen)
                    except StopIteration:
                        gen_live = False

            NIT = NJF * E
            ev_u1_hist = {}
            ev_u3_hist = {}
            ev_s1_hist = {}
            ev_pr_hist = {}
            ev_sa_hist = {}
            ev_mixmm_hist = {}
            ev_cw1_ev_by_j = {}
            ev_cw3_ev_by_j = {}
            ev_mixev_by_j = {}
            ev_zmm_by_j = {}
            ev_cwdma = {}
            ev_exdma = {}
            ev_cw1mm_by_j = {}
            ev_cw3mm_by_j = {}
            ev_z_last = None
            ev_mix_last = None

            def emit_cw_dma(j):
                w1_ = []
                if j >= 2:
                    w1_.append(ev_cw3mm_by_j[j - 2])
                elif ev_wj_used[j % 2] is not None:
                    w1_.append(ev_wj_used[j % 2])
                if j == 0:
                    w1_.append(ev_const_g)
                ev_cwdma[j] = GD(lambda e, j=j: e.dma_start(out=wjb[:, j % 2, 0:2048],
                                                            in_=wjp[j][:, 0:2048]), waits=w1_)

            def emit_ex_dma(j):
                w2_ = []
                if j >= 2:
                    w2_.append(ev_zmm_by_j[j - 2])
                elif ev_wj_used[j % 2] is not None:
                    w2_.append(ev_wj_used[j % 2])
                ev_exdma[j] = GD(lambda e, j=j: e.dma_start(out=wjb[:, j % 2, 2048:WJ_COLS],
                                                            in_=wjp[j][:, 2048:WJ_COLS]), waits=w2_)

            def emit_cw_mm(j, which, k0):
                bank = j % 2
                off = OFF_WG if which == 1 else OFF_WU
                for kd in range(k0, k0 + 4):
                    w = []
                    if kd == 0:
                        w.append(ev_cwdma[j])
                        if j == 0:
                            w.append(ev_xnT)
                        if which == 1 and j >= 2:
                            w.append(ev_mixev_by_j[j - 2])
                        if which == 3:
                            w.append(ev_cw1_ev_by_j[j])
                    TT(lambda e, kd=kd, j=j, bank=bank, off=off:
                       e.matmul(pb[bank][:, 0:T], wjb[:, j % 2, off + kd * 128:off + (kd + 1) * 128],
                                xnT[:, kd, :], start=(kd == 0), stop=(kd == NKD - 1)),
                       waits=w)
                ev = S.cur(st)
                if which == 1:
                    ev_cw1mm_by_j[j] = ev
                else:
                    ev_cw3mm_by_j[j] = ev

            def emit_cw_evac(j, which):
                mm = ev_cw1mm_by_j[j] if which == 1 else ev_cw3mm_by_j[j]
                buf = cw1s if which == 1 else cw3s
                ev = SC(lambda e, j=j, buf=buf: e.activation(buf[:, j % 2, :], pb[j % 2][:, 0:T],
                                                             AF.Identity),
                        waits=[mm])
                if which == 1:
                    ev_cw1_ev_by_j[j] = ev
                else:
                    ev_cw3_ev_by_j[j] = ev
                if debug and g == 0 and j == 0:
                    nm = "dbg_cw1" if which == 1 else "dbg_cw3"
                    GD(lambda e, buf=buf, nm=nm: e.dma_start(out=dbg[nm], in_=buf[:, 0, :]), waits=[ev])

            emit_cw_dma(0)
            emit_ex_dma(0)
            emit_cw_dma(1)
            emit_ex_dma(1)
            emit_cw_mm(0, 1, 0)
            emit_cw_mm(0, 1, 4)
            emit_cw_evac(0, 1)
            emit_cw_mm(0, 3, 0)
            emit_cw_mm(0, 3, 4)
            emit_cw_evac(0, 3)

            for k in range(NIT + 4):
                in_range = k < NIT
                if in_range:
                    jf, ee = divmod(k, E)
                    par = jf % 2
                    ep = ee % 2
                    qb = 64 * (ee // 4)
                    d1b = 2 + ep
                    d3b = 4 + ep
                    if jf + 1 < NJF:
                        if ee == 4:
                            emit_cw_mm(jf + 1, 1, 0)
                        elif ee == 5:
                            emit_cw_mm(jf + 1, 1, 4)
                        elif ee == 6:
                            emit_cw_evac(jf + 1, 1)
                            emit_cw_mm(jf + 1, 3, 0)
                        elif ee == 7:
                            emit_cw_mm(jf + 1, 3, 4)
                    if ee == 0 and jf >= 1:
                        emit_cw_evac(jf, 3)
                    if ee == 6 and jf + 2 < NJF:
                        emit_cw_dma(jf + 2)
                    if ee == 4 and jf >= 1 and jf + 1 < NJF:
                        emit_ex_dma(jf + 1)
                    w_d1 = []
                    if k == 0:
                        w_d1 = [ev_tst]
                    if k >= 2 and (k - 2) in ev_u1_hist:
                        w_d1 = list(w_d1) + [ev_u1_hist[k - 2]]
                    ev_d1 = TT(lambda e, par=par, ee=ee, qb=qb, b=d1b:
                               e.matmul(pb[b][:, 0:T],
                                        wjb[qb:qb + 64, par, OFF_B1 + 128 * ee:OFF_B1 + 128 * (ee + 1)],
                                        t1s[qb:qb + 64, :], start=True, stop=True),
                               waits=w_d1 + ([ev_exdma[jf]] if ee == 0 else []))
                    ev_u1_hist[(k, "d")] = ev_d1
                    w_d3 = []
                    if k >= 2 and (k - 2) in ev_u3_hist:
                        w_d3 = [ev_u3_hist[k - 2]]
                    ev_d3 = TT(lambda e, par=par, ee=ee, qb=qb, b=d3b:
                               e.matmul(pb[b][:, 0:T],
                                        wjb[qb:qb + 64, par, OFF_B3 + 128 * ee:OFF_B3 + 128 * (ee + 1)],
                                        t3s[qb:qb + 64, :], start=True, stop=True),
                               waits=w_d3)
                    ev_u3_hist[(k, "d")] = ev_d3
                    if k == 0:
                        emit_wgt_t()
                    elif k == 1:
                        emit_wbc()
                m = k - 1
                if 0 <= m < NIT:
                    jfm, eem = divmod(m, E)
                    epm = m % 2
                    w_u1 = [ev_u1_hist[(m, "d")], ev_cw1_ev_by_j[jfm]]
                    if m >= 2 and (m - 2) in ev_pr_hist:
                        w_u1.append(ev_pr_hist[m - 2])
                    ev_u1 = V(lambda e, epm=epm, b=2 + epm, jp=jfm % 2: e.scalar_tensor_tensor(
                        u1b[:, epm, :], pb[b][:, 0:T], 2.0, cw1s[:, jp, :], ALU.mult, ALU.add),
                        waits=w_u1)
                    ev_u1_hist[m] = ev_u1
                    ev_u3 = V(lambda e, epm=epm, b=4 + epm, jp=jfm % 2: e.scalar_tensor_tensor(
                        u3b[:, epm, :], pb[b][:, 0:T], 2.0, cw3s[:, jp, :], ALU.mult, ALU.add),
                        waits=[ev_u3_hist[(m, "d")], ev_cw3_ev_by_j[jfm]])
                    ev_u3_hist[m] = ev_u3
                    w_s1 = [ev_u1]
                    if m >= 2 and (m - 2) in ev_pr_hist:
                        w_s1.append(ev_pr_hist[m - 2])
                    if debug and g == 0 and m == 0:
                        w_s1.append(GD(lambda e: e.dma_start(out=dbg["dbg_u1"], in_=u1b[:, 0, :]),
                                       waits=[ev_u1]))
                    ev_s1 = SC(lambda e, epm=epm: e.activation(u1b[:, epm, :], u1b[:, epm, :], AF.Silu),
                               waits=w_s1)
                    ev_s1_hist[m] = ev_s1
                m = k - 2
                if 0 <= m < NIT:
                    epm = m % 2
                    sp = m % 3
                    w_pr = [ev_s1_hist[m], ev_u3_hist[m]]
                    if m >= 3 and (m - 3) in ev_sa_hist:
                        w_pr.append(ev_sa_hist[m - 3])
                    if m >= 3 and (m - 3) in ev_mixmm_hist:
                        w_pr.append(ev_mixmm_hist[m - 3])
                    ev_pr = V(lambda e, epm=epm, sp=sp: e.tensor_tensor(
                        sab[:, sp, :], u1b[:, epm, :], u3b[:, epm, :], ALU.mult),
                        waits=w_pr)
                    ev_pr_hist[m] = ev_pr
                    w_sa = [ev_pr]
                    if m == 0:
                        w_sa.append(wbc_state["wbc"])
                    ev_sa = GC(lambda e, sp=sp, eem=m % E: e.tensor_tensor(
                        sab[:, sp, :], sab[:, sp, :], wbc[:, eem, :], ALU.mult),
                        waits=w_sa)
                    ev_sa_hist[m] = ev_sa
                    if debug and g == 0 and m == 0:
                        ev_sa_hist[m] = GD(lambda e: e.dma_start(out=dbg["dbg_sact"], in_=sab[:, 0, :]),
                                           waits=[ev_sa])
                m = k - 4
                if 0 <= m < NIT:
                    jfm, eem = divmod(m, E)
                    sp = m % 3
                    lo_m = eem < 4
                    eo_m = eem % 4
                    ev_z = TT(lambda e, eem=eem, eo=eo_m, par=jfm % 2, b=6 if lo_m else 7, sp=sp, jfm=jfm:
                              e.matmul(pb[b][32 * eo:32 * eo + 32, 0:T],
                                       wjb[:, par, OFF_A2 + eem * 32:OFF_A2 + (eem + 1) * 32],
                                       sab[:, sp, :], start=(jfm == 0), stop=(jfm == NJF - 1),
                                       tile_position=(0, 32 * eo), skip_group_check=True),
                              waits=[ev_sa_hist[m]])
                    ev_z_last = ev_z
                    w_mix = []
                    if eem == 0:
                        w_mix = [ev_cw3_ev_by_j[jfm]]
                    ev_mix = TT(lambda e, sp=sp, eem=eem, bank=jfm % 2:
                                e.matmul(pb[bank][:, 0:T], idb[:, :], sab[:, sp, :],
                                         start=(eem == 0), stop=(eem == E - 1)),
                                waits=w_mix)
                    ev_mixmm_hist[m] = ev_mix
                    if eem == E - 1:
                        ev_zmm_by_j[jfm] = ev_z
                        ev_mixev = V(lambda e, jfm=jfm, bank=jfm % 2:
                                     e.tensor_copy(big[:, jfm, :], pb[bank][:, 0:T]),
                                     waits=[ev_mix])
                        ev_mixev_by_j[jfm] = ev_mixev
                        ev_mix_last = ev_mixev
                if k % 2 == 0:
                    pop_filler()
                if g == 0 and conv_state["live"] and k % 12 == 1:
                    try:
                        next(conv_state["gen"])
                    except StopIteration:
                        conv_state["live"] = False
            while gen_live:
                pop_filler()
            ev_wj_used[0] = ev_mix_last
            ev_wj_used[1] = ev_mix_last
            if debug and g == 0:
                GD(lambda e: e.dma_start(out=dbg["dbg_mix"], in_=big[:, :, :]), waits=[ev_mix_last])

            ev_zlo = V(lambda e: e.tensor_scalar(zlos[:, :], pb[6][:, 0:T], 2.0, None, ALU.mult),
                       waits=[ev_z_last])
            ev_zhi = V(lambda e: e.tensor_scalar(zhis[:, :], pb[7][:, 0:T], 2.0, None, ALU.mult),
                       waits=[ev_z_last])
            if debug and g == 0:
                GD(lambda e: e.dma_start(out=dbg["dbg_zlo"], in_=zlos[:, :]), waits=[ev_zlo])
                GD(lambda e: e.dma_start(out=dbg["dbg_zhi"], in_=zhis[:, :]), waits=[ev_zhi])

            ev_mm_wd = {}
            ev_wdbuf = [None] * 4
            for kf in range(NKF):
                parw = kf % 2
                w = []
                if kf >= 2:
                    w.append(ev_mm_wd[kf - 2])
                else:
                    w.append(ev_wsb_guard if ev_wsb_guard is not None else ev_mix_last)
                if g == 0:
                    ev_wdbuf[parw] = GD(lambda e, kf=kf, p=parw:
                                        e.dma_start(out=wsb[:, p, :], in_=wdp[kf]), waits=w)
                else:
                    w.append(CONV_WDW2_DONE)
                    ev_wdbuf[parw] = YD(lambda e, kf=kf, p=parw:
                                        e.dma_start(out=wsb[:, p, :], in_=wdq[kf]), waits=w)
                for tc in range(NC_):
                    for hh in range(2):
                        w2_ = []
                        if tc == 0 and hh == 0:
                            w2_.append(ev_wdbuf[parw])
                            if kf == 0:
                                w2_ += [ev_mix_last, ev_zlo, ev_zhi]
                        ev_mm = TT(lambda e, kf=kf, tc=tc, hh=hh, p=parw:
                                   e.matmul(pb[2 * tc + hh][:, 0:T],
                                            big[:, kf, tc * 128:(tc + 1) * 128],
                                            wsb[:, p, hh * 512:(hh + 1) * 512],
                                            start=(kf == 0), stop=False),
                                   waits=w2_)
                ev_mm_wd[kf] = ev_mm
            for tc in range(NC_):
                for hh in range(2):
                    TT(lambda e, tc=tc, hh=hh: e.matmul(pb[2 * tc + hh][:, 0:T],
                                                        zlos[:, tc * 128:(tc + 1) * 128],
                                                        smb[:, OFF_B2LO + hh * 512:OFF_B2LO + (hh + 1) * 512],
                                                        start=False, stop=False),
                       waits=[ev_zlo] if (tc == 0 and hh == 0) else ())
                    TT(lambda e, tc=tc, hh=hh: e.matmul(pb[2 * tc + hh][:, 0:T],
                                                        zhis[:, tc * 128:(tc + 1) * 128],
                                                        smb[:, OFF_B2HI + hh * 512:OFF_B2HI + (hh + 1) * 512],
                                                        start=False, stop=True),
                       waits=[ev_zhi] if (tc == 0 and hh == 0) else ())
            ev_ffnmm = S.cur(st)
            ev_fftap = None
            ev_si_b23 = None
            for tc in range(NC_):
                for hh in range(2):
                    w_si = []
                    if tc == 0 and hh == 0:
                        w_si = [ev_ffnmm, fdone["x1c"]]
                    SC(lambda e, tc=tc, hh=hh: e.activation(scrB[:, tc, hh * 512:(hh + 1) * 512],
                                                            pb[2 * tc + hh][:, 0:T], AF.Silu),
                       waits=w_si)
                    if tc == 1 and hh == 1:
                        ev_si_b23 = S.cur(ss)
            ev_g = S.cur(ss)
            if debug and g == 0:
                ev_gtap = OD(lambda e: e.dma_start(out=dbg["dbg_g"], in_=scrB[:, :, :]), waits=[ev_g])
            else:
                ev_gtap = None

            ev_qev = []
            for c in range(NC_):
                b = 2 + c % 2
                waits = [fdone["q"], ev_si_b23] if c == 0 else []
                if len(ev_qev) >= 2:
                    waits.append(ev_qev[-2])
                ev_t = TT(lambda e, c=c, b=b: e.transpose(pb[b][0:16, 0:128], qs1[:, c, :], idf[:, :]),
                          waits=waits)
                ev_e = V(lambda e, c=c, b=b: e.tensor_copy(qT[:, c * 128:(c + 1) * 128], pb[b][0:16, 0:128]),
                         waits=[ev_t])
                ev_qev.append(ev_e)
            ev_qT = S.cur(sv)
            ev_h = []
            for jf in range(NJF):
                b = 2 + jf % 2
                waits = [ev_qT, ev_const_g] if jf == 0 else []
                if len(ev_h) >= 2:
                    waits.append(ev_h[-2])
                ev_m = TT(lambda e, jf=jf, b=b: e.matmul(pb[b][:, 0:T],
                                                         w1b[:, jf * 128:(jf + 1) * 128],
                                                         qT[:, :], start=True, stop=True),
                          waits=waits)
                w_r = [ev_m]
                if jf == 0:
                    w_r.append(ev_ffnmm)
                ev_r = SC(lambda e, jf=jf, b=b: e.activation(big[:, jf, :], pb[b][:, 0:T],
                                                             AF.Relu, bias=b1c[:, jf:jf + 1], scale=1.0),
                          waits=w_r)
                ev_h.append(ev_r)
            ev_rT = S.cur(ss)
            for c in range(NC_):
                V(lambda e, c=c: e.tensor_add(x1[:, c, :], x1[:, c, :], b2bc[:, :]),
                  waits=[ev_qT] if c == 0 else ())
            ev_x1b = S.cur(sv)
            ev_mm_by_kf = {}
            ev_w2buf = [None] * 4
            for kf in range(NKF):
                parw = kf % 2
                waits = []
                if kf >= 2:
                    waits.append(ev_mm_by_kf[kf - 2])
                else:
                    waits.append(ev_mm_wd[NKF - 1])
                if g == 0:
                    ev_w2buf[parw] = GD(lambda e, kf=kf, p=parw:
                                        e.dma_start(out=wsb[:, p, :], in_=w2p[kf]), waits=waits)
                else:
                    waits.append(CONV_WDW2_DONE)
                    ev_w2buf[parw] = YD(lambda e, kf=kf, p=parw:
                                        e.dma_start(out=wsb[:, p, :], in_=w2q[kf]), waits=waits)
                for tc in range(NC_):
                    for hh in range(2):
                        w2_ = []
                        if tc == 0 and hh == 0:
                            w2_.append(ev_w2buf[parw])
                            if kf == 0:
                                w2_ += [ev_rT, ev_g]
                        ev_mm = TT(lambda e, kf=kf, tc=tc, hh=hh, p=parw:
                                   e.matmul(pb[2 * tc + hh][:, 0:T],
                                            big[:, kf, tc * 128:(tc + 1) * 128],
                                            wsb[:, p, hh * 512:(hh + 1) * 512],
                                            start=(kf == 0), stop=(kf == NKF - 1)),
                                   waits=w2_)
                ev_mm_by_kf[kf] = ev_mm
            ev_ffmm = S.cur(st)
            ev_wsb_guard = ev_ffmm
            ev_s2 = []
            for c in range(NC_):
                for hh in range(2):
                    w_e = []
                    if c == 0 and hh == 0:
                        w_e = [ev_ffmm, ev_x1b]
                        if ev_fftap is not None:
                            w_e.append(ev_fftap)
                    ev_e = V(lambda e, c=c, hh=hh:
                             e.tensor_tensor(x1[:, c, hh * 512:(hh + 1) * 512],
                                             pb[2 * c + hh][:, 0:T],
                                             x1[:, c, hh * 512:(hh + 1) * 512], ALU.add),
                             waits=w_e)
                    ev_s2.append(ev_e)
            ev_ff = S.cur(sv)
            ev_banks67_prev = ev_ff
            if debug and g == 0:
                OD(lambda e: e.dma_start(out=dbg["dbg_ff"], in_=x1[:, :, :]), waits=[ev_ff])
            base_sv = S.cnt.get(sv, 0)
            n_warm = 18 if g + 1 < G else 14
            for i in range(n_warm):
                TT(lambda e: e.matmul(pb[0][0:8, 0:128], idf[:, 0:8], idf[:, :],
                                      start=True, stop=True, skip_group_check=True),
                   waits=[(sv, base_sv + 3 * (i + 1))], inc=False)
            for c in range(NC_):
                V(lambda e, c=c: e.scalar_tensor_tensor(sfin[:, c, :], x1[:, c, :], 1.0,
                                                        x1[:, c, :], ALU.mult, ALU.mult,
                                                        accum_out=stats[:, 8 + c:9 + c]))
            for c in range(NC_):
                V(lambda e, c=c: e.reduce_sum(stats[:, c:c + 1], x1[:, c, :], axis=AX))
            for c in range(NC_):
                V(lambda e, c=c: e.tensor_scalar(stats[:, 4 + c:5 + c], stats[:, c:c + 1],
                                                 1.0 / D, None, ALU.mult))
            for c in range(NC_):
                V(lambda e, c=c: e.scalar_tensor_tensor(stats[:, 12 + c:13 + c],
                                                        stats[:, 4 + c:5 + c], -float(D),
                                                        stats[:, 4 + c:5 + c], ALU.mult, ALU.mult))
            for c in range(NC_):
                V(lambda e, c=c: e.tensor_add(stats[:, 8 + c:9 + c], stats[:, 8 + c:9 + c],
                                              stats[:, 12 + c:13 + c]))
            ev_v2 = S.cur(sv)
            for c in range(NC_):
                SC(lambda e, c=c: e.activation(stats[:, 12 + c:13 + c], stats[:, 8 + c:9 + c],
                                               AF.Sqrt, bias=cst[:, C_EPS:C_EPS + 1], scale=1.0 / D),
                   waits=[ev_v2] if c == 0 else ())
            ev_sd2 = S.cur(ss)
            for c in range(NC_):
                V(lambda e, c=c: e.reciprocal(stats[:, 12 + c:13 + c], stats[:, 12 + c:13 + c]),
                  waits=[ev_sd2] if c == 0 else ())
            V(lambda e: e.drain(), inc=False)
            for c in range(NC_):
                V(lambda e, c=c: e.tensor_scalar(sfin[:, c, :], x1[:, c, :],
                                                 stats[:, 4 + c:5 + c], stats[:, 12 + c:13 + c],
                                                 ALU.subtract, ALU.mult))
            for c in range(NC_):
                V(lambda e, c=c: e.tensor_mul(sfin[:, c, :], sfin[:, c, :], g2b[:, :]))
            for c in range(NC_):
                V(lambda e, c=c: e.tensor_add(sfin[:, c, :], sfin[:, c, :], be2b[:, :]))
            ev_s = S.cur(sv)
            if debug and g == 0:
                OD(lambda e: e.dma_start(out=dbg["dbg_s"], in_=sfin[:, :, :]), waits=[ev_s])

            xout = scrA
            for c in range(NC_):
                w_cb = [ev_s, ev_g] if c == 0 else []
                if c == 0 and ev_gtap is not None:
                    w_cb.append(ev_gtap)
                V(lambda e, c=c: e.tensor_tensor(xout[:, c, :], scrB[:, c, :], sfin[:, c, :], ALU.mult),
                  waits=w_cb)
            for c in range(NC_):
                V(lambda e, c=c: e.tensor_tensor(xout[:, c, :], xout[:, c, :], xr[:, c, :], ALU.add))
            ev_outv = S.cur(sv)
            orow = od[g * T:(g + 1) * T, :].rearrange("(p c) d -> p c d", c=NC_)
            ev_odma = OD(lambda e, r=orow: e.dma_start(out=r, in_=xout[:, :, :]), waits=[ev_outv])
            ev_odma_prev = ev_odma
            ev_scrb_guard = ev_gtap
            ev_outv_prev = ev_outv

        S.items["sync"].append((lambda e: None, [S.cur(so), S.cur(sg), S.cur(sqw), S.cur(sx)], None))

        with nc.Block() as block:
            @block.sync
            def _(eng):
                for fn, waits, ev in S.items["sync"]:
                    for (sem, val) in waits:
                        eng.wait_ge(sem, val)
                    inst = fn(eng)
                    if ev is not None:
                        inst.then_inc(ev[0], 1)

            @block.gpsimd
            def _(eng):
                S.emit("gpsimd", eng)

            @block.tensor
            def _(eng):
                S.emit("tensor", eng)

            @block.vector
            def _(eng):
                S.emit("vector", eng)

            @block.scalar
            def _(eng):
                S.emit("scalar", eng)

    return nc


def _prep_host(inputs):
    w_gate = np.asarray(inputs["L0_ffn_w_gate"], np.float32)
    w_up = np.asarray(inputs["L0_ffn_w_up"], np.float32)
    w_down = np.asarray(inputs["L0_ffn_w_down"], np.float32)
    w2 = np.asarray(inputs["L0_attn_w2"], np.float32)
    w1 = np.asarray(inputs["L0_attn_w1"], np.float32)
    gate_w = np.asarray(inputs["L0_ffn_gate_w"], np.float32)
    A1 = np.asarray(inputs["L0_ffn_A1"], np.float32)
    B1 = np.asarray(inputs["L0_ffn_B1"], np.float32)
    A3 = np.asarray(inputs["L0_ffn_A3"], np.float32)
    B3 = np.asarray(inputs["L0_ffn_B3"], np.float32)
    A2 = np.asarray(inputs["L0_ffn_A2"], np.float32)
    B2 = np.asarray(inputs["L0_ffn_B2"], np.float32)

    wjp = np.zeros((NJF, 128, WJ_COLS), np.float32)
    wg4 = w_gate.reshape(NJF, 128, NKD, 128).transpose(0, 3, 2, 1)
    wjp[:, :, OFF_WG:OFF_WG + 1024] = wg4.reshape(NJF, 128, NKD * 128)
    wu4 = w_up.reshape(NJF, 128, NKD, 128).transpose(0, 3, 2, 1)
    wjp[:, :, OFF_WU:OFF_WU + 1024] = wu4.reshape(NJF, 128, NKD * 128)
    b1r = B1.reshape(E, NJF, 128, R).transpose(0, 1, 3, 2)
    b3r = B3.reshape(E, NJF, 128, R).transpose(0, 1, 3, 2)
    for ee in range(E):
        rr0 = 64 * (ee // 4) + 16 * (ee % 4)
        wjp[:, rr0:rr0 + 16, OFF_B1 + 128 * ee:OFF_B1 + 128 * (ee + 1)] = b1r[ee]
        wjp[:, rr0:rr0 + 16, OFF_B3 + 128 * ee:OFF_B3 + 128 * (ee + 1)] = b3r[ee]
    a2r = A2.reshape(E, R, NJF, 128).transpose(2, 3, 0, 1)
    a2p = np.zeros((NJF, 128, E, 32), np.float32)
    a2p[:, :, :, :R] = a2r
    wjp[:, :, OFF_A2:OFF_A2 + 256] = a2p.reshape(NJF, 128, 256)

    w2p = np.ascontiguousarray(w2.T.reshape(NKF, 128, 1024))
    wdp = np.ascontiguousarray(w_down.T.reshape(NKF, 128, 1024))

    spk = np.zeros((128, SP_COLS), np.float32)

    def astack(Ax):
        tight = Ax.reshape(E * R, D)
        return tight.T.reshape(NKD, 128, 128).transpose(1, 0, 2).reshape(128, NKD * 128)

    spk[:, OFF_A1:OFF_A1 + 1024] = astack(A1)
    spk[:, OFF_A3:OFF_A3 + 1024] = astack(A3)
    for eo in range(4):
        spk[32 * eo:32 * eo + 16, OFF_B2LO:OFF_B2LO + 1024] = B2[eo].T
        spk[32 * eo:32 * eo + 16, OFF_B2HI:OFF_B2HI + 1024] = B2[4 + eo].T
    gt = gate_w.T.reshape(NKD, 128, 8).transpose(1, 0, 2).reshape(128, NKD * 8)
    spk[:, OFF_GATE:OFF_GATE + 64] = gt

    w1tp = w1.reshape(NJF, 128, R).transpose(2, 0, 1).reshape(16, FF).copy()

    b1c = np.asarray(inputs["L0_attn_b1"], np.float32).reshape(32, 128).T.copy()
    b2c = np.ascontiguousarray(np.broadcast_to(np.asarray(inputs["L0_attn_b2"], np.float32), (128, D)))
    g1b = np.ascontiguousarray(np.broadcast_to(np.asarray(inputs["L0_attn_gamma1"], np.float32), (128, D)))
    be1b = np.ascontiguousarray(np.broadcast_to(np.asarray(inputs["L0_attn_beta1"], np.float32), (128, D)))
    g2b = np.ascontiguousarray(np.broadcast_to(np.asarray(inputs["L0_attn_gamma2"], np.float32), (128, D)))
    be2b = np.ascontiguousarray(np.broadcast_to(np.asarray(inputs["L0_attn_beta2"], np.float32), (128, D)))
    idf = np.eye(128, dtype=np.float32)
    sel = np.zeros((8, 8, 128), np.float32)
    for ee in range(8):
        sel[ee, ee, :] = 1.0
    sel = sel.reshape(8, 1024)
    cst = np.zeros((128, 8), np.float32)
    cst[:, C_EPS] = EPS
    cst[:, C_PIH] = SIN_BIAS
    cst[:, C_QTR] = 0.25
    cst[:, C_MAG] = MAGIC
    cst[:, C_NMAG] = -MAGIC

    return dict(wjp=wjp, w2p=w2p, wdp=wdp, spk=spk, w1t=w1tp, b1d=b1c, b2d=b2c,
                g1d=g1b, be1d=be1b, g2d=g2b, be2d=be2b, idfd=idf,
                seld=sel, cstd=cst, gtd=np.ascontiguousarray(gt))


def kernel(**inputs):
    assert inputs["L0_attn_attn_circuit_params"] is not None
    assert inputs["L0_attn_ffn_circuit_params"] is not None
    x = np.ascontiguousarray(np.asarray(inputs["L0_attn_x"], np.float32).reshape(16384, D))
    shared = _prep_host(inputs)
    nc = build(debug=False)
    in_maps = []
    for i in range(8):
        m = dict(shared)
        xc = x[i * PC:(i + 1) * PC]
        m["x"] = np.ascontiguousarray(
            xc.reshape(G, NC_, 128, D).transpose(0, 2, 1, 3).reshape(PC, D))
        in_maps.append(m)
    res = run_bass_kernel_spmd(nc, in_maps, core_ids=list(range(8)))
    outs = []
    for i in range(8):
        oc = np.asarray(res.results[i]["o"])
        outs.append(oc.reshape(G, 128, NC_, D).transpose(0, 2, 1, 3).reshape(PC, D))
    out = np.concatenate(outs, axis=0)
    return out.reshape(4, 4096, D)
